# version 01
# baseline (speedup 1.0000x reference)
import numpy as np
import ml_dtypes
import concourse.bass as bass
import concourse.mybir as mybir
from concourse.bass_utils import run_bass_kernel_spmd
from contextlib import ExitStack

F32 = mybir.dt.float32
BF16 = mybir.dt.bfloat16
F16 = mybir.dt.float16
F8 = mybir.dt.float8e4
OP = mybir.AluOpType
AF = mybir.ActivationFunctionType
AX = mybir.AxisListType.X

D = 1024
HD = 64
NCORE = 8
NEG = -1.0e30
_SWAP16 = [(i + 16) % 32 for i in range(32)]



def _head_perm(interleaved):
    perm = np.empty(64, dtype=np.int64)
    for p in range(64):
        qd, r = divmod(p, 32)
        j = qd * 16 + (r % 16)
        first = (r % 32) < 16
        perm[p] = (2 * j + (0 if first else 1)) if interleaved else (j + (0 if first else 32))
    return perm


def _full_perm(nrows, interleaved):
    hp = _head_perm(interleaved)
    return np.concatenate([h * 64 + hp for h in range(nrows // 64)])


def _rope_tables(T):
    inv_freq = 1.0 / (10000.0 ** (np.arange(0, HD, 2, dtype=np.float64) / HD))
    t = np.arange(T, dtype=np.float64)
    ang = np.outer(inv_freq, t)
    cos, sin = np.cos(ang), np.sin(ang)
    C = np.empty((64, T)); S0 = np.empty((64, T)); S1 = np.empty((64, T))
    for p in range(64):
        qd, r = divmod(p, 32)
        j = qd * 16 + (r % 16)
        first = (r % 32) < 16
        C[p] = cos[j]
        S0[p] = (-sin[j]) if first else sin[j]
        S1[p] = sin[j] if first else (-sin[j])
    tile = lambda M: np.ascontiguousarray(np.tile(M, (2, 1)), dtype=np.float32)
    return tile(C), tile(S0), tile(S1)


def _causal_masks():
    m = np.zeros((128, 4 * 512), dtype=np.float32)
    kl = np.arange(128)[:, None]
    ql = np.arange(512)[None, :]
    for d in range(4):
        m[:, 512 * d:512 * (d + 1)] = np.where(kl + 128 * d <= ql, 0.0, NEG)
    return m


def _host_prepare(T, inputs):
    g = {}
    f32 = lambda a: np.ascontiguousarray(np.asarray(a), dtype=np.float32)
    p0 = _full_perm(D, True)
    p1 = _full_perm(D, False)
    p1k = _full_perm(256, False)
    g["wq0"] = np.ascontiguousarray(f32(inputs["L0_attn_Wq"]).T[:, p0])
    g["wk0"] = np.ascontiguousarray(f32(inputs["L0_attn_Wk"]).T[:, p0])
    g["wv0"] = np.ascontiguousarray(f32(inputs["L0_attn_Wv"]).T)
    g["wo0"] = np.ascontiguousarray(f32(inputs["L0_attn_Wo"]).T)
    g["bqkv0"] = np.ascontiguousarray(np.stack([
        f32(inputs["L0_attn_bq"])[p0], f32(inputs["L0_attn_bk"])[p0],
        f32(inputs["L0_attn_bv"])], 0).reshape(3, 8, 128).transpose(2, 0, 1))
    g["bo0"] = np.ascontiguousarray(f32(inputs["L0_attn_bo"]).reshape(8, 128).T)
    g["lnw0"] = f32(inputs["L0_attn_ln_w"]).reshape(1, D)
    g["lnb0"] = f32(inputs["L0_attn_ln_b"]).reshape(1, D)
    g["fk0"] = np.ascontiguousarray(np.asarray(inputs["L0_ffn_kernel"]))
    g["fs_in"] = f32(inputs["L0_ffn_input_scale"]).reshape(1, 1)
    g["fs_k"] = f32(inputs["L0_ffn_kernel_scale"]).reshape(1, 1)
    g["fb0"] = np.ascontiguousarray(np.asarray(inputs["L0_ffn_bias"]).reshape(8, 128).T)
    g["wq1"] = np.ascontiguousarray(f32(inputs["L1_attn_Wq"])[p1, :])
    g["wk1"] = np.ascontiguousarray(f32(inputs["L1_attn_Wk"])[p1k, :])
    g["wv1"] = f32(inputs["L1_attn_Wv"])
    g["wo1"] = f32(inputs["L1_attn_Wo"])
    g["qA"] = f32(inputs["L1_attn_qA"])
    g["qB"] = np.ascontiguousarray(f32(inputs["L1_attn_qB"])[:, p1])
    g["kA"] = f32(inputs["L1_attn_kA"])
    g["kB"] = np.ascontiguousarray(f32(inputs["L1_attn_kB"])[:, p1k])
    g["vA"] = f32(inputs["L1_attn_vA"])
    g["vB"] = f32(inputs["L1_attn_vB"])
    g["pA"] = f32(inputs["L1_attn_pA"])
    g["pB"] = f32(inputs["L1_attn_pB"])
    g["qgain"] = np.ascontiguousarray(np.repeat(f32(inputs["L1_attn_q_gain"]), 64).reshape(8, 128).T)
    g["hgate"] = np.ascontiguousarray(np.tile(f32(inputs["L1_attn_head_gate"])[None, :], (64, 1)))
    g["rW"] = f32(inputs["L1_ffn_router_W"])
    g["rb"] = f32(inputs["L1_ffn_router_b"]).reshape(1, 8)
    g["W1"] = f32(inputs["L1_ffn_W1"]).reshape(8 * 1024, 4096)
    g["b1"] = np.ascontiguousarray(
        f32(inputs["L1_ffn_b1"]).reshape(8, 32, 128).transpose(2, 0, 1).reshape(128, 256))
    g["W2"] = f32(inputs["L1_ffn_W2"]).reshape(8 * 4096, 1024)
    g["b2"] = f32(inputs["L1_ffn_b2"])
    C, S0, S1 = _rope_tables(T)
    g["ctab"], g["s0tab"], g["s1tab"] = C, S0, S1
    g["cmask"] = _causal_masks().astype(ml_dtypes.bfloat16)
    g["idf"] = np.eye(128, dtype=np.float32)
    g["idb"] = np.eye(128, dtype=ml_dtypes.bfloat16)
    vones = np.zeros((128, 4 * 130), dtype=ml_dtypes.bfloat16)
    for i in range(4):
        vones[:, i * 130 + 64] = 1.0
    g["vones"] = vones
    cvec = np.zeros((128, 8), dtype=np.float32)
    cvec[:, 0] = 1.0 / D
    cvec[:, 1] = 1e-5
    cvec[:, 2] = 0.125
    cvec[:, 3] = 1.0
    g["cvec"] = cvec
    return g



class _Tok:
    __slots__ = ("q", "idx", "used", "value", "is_dma")

    def __init__(self, q, idx):
        self.q, self.idx, self.used, self.value = q, idx, False, None


class _EngQ:

    def __init__(self, name, step):
        self.name, self.step, self.items = name, step, []
        self.chans = [self]

    def op(self, fn, after=(), chan=None, is_dma=None):
        deps = []
        for d in (after if isinstance(after, (list, tuple)) else (after,)):
            if d is not None:
                d.used = True
                deps.append(d)
        tok = _Tok(chan or self, len(self.items))
        tok.is_dma = (self.step == 16) if is_dma is None else is_dma
        self.items.append((fn, deps, tok))
        return tok


class _Chan:

    def __init__(self, parent, name):
        self.parent, self.name, self.step = parent, name, parent.step
        parent.chans.append(self)

    def op(self, fn, after=()):
        return self.parent.op(fn, after=after, chan=self)


class Prog:
    def __init__(self):
        self.pe = _EngQ("pe", 1)
        self.sc = _EngQ("sc", 1)
        self.ve = _EngQ("ve", 1)
        self.sy = _EngQ("sy", 16)
        self.gp = _EngQ("gp", 16)
        self.qs = [self.pe, self.sc, self.ve, self.sy, self.gp]

    def chan(self, queue, name):
        return _Chan(queue, name)

    def barrier(self):
        tails = []
        for q in (self.pe, self.sc, self.ve):
            if q.items:
                tails.append(q.items[-1][2])
        for q in (self.sy, self.gp):
            seen = set()
            for fn, deps, tok in reversed(q.items):
                cn = tok.q.name
                if cn not in seen:
                    seen.add(cn)
                    tails.append(tok)
        return self.ve.op(lambda v: v.nop(), after=tails)

    def finalize(self):
        for q in (self.sy, self.gp):
            for fn, deps, tok in q.items:
                if tok.is_dma:
                    tok.used = True
        counters = {}
        for q in self.qs:
            for fn, deps, tok in q.items:
                c = tok.q
                if tok.used:
                    counters[c.name] = counters.get(c.name, 0) + c.step
                tok.value = counters.get(c.name, 0)
        for q in (self.sy, self.gp):
            runs = {}
            for fn, deps, tok in q.items:
                if not tok.is_dma:
                    continue
                cn = tok.q.name
                if deps or cn not in runs:
                    runs[cn] = [tok]
                else:
                    runs[cn].append(tok)
                if deps:
                    runs[cn] = [tok]
            runs = {}
            order = []
            for fn, deps, tok in q.items:
                if not tok.is_dma:
                    continue
                cn = tok.q.name
                if deps or cn not in runs:
                    runs[cn] = []
                    order.append(runs[cn])
                runs[cn].append(tok)
            for run in order:
                endv = run[-1].value
                for tok in run:
                    tok.value = endv

    def emit(self, block, sems):
        def run(q, eng):
            maxw = {}
            prev_end = {}
            for fn, deps, tok in q.items:
                need = {}
                for d in deps:
                    need[d.q.name] = max(need.get(d.q.name, 0), d.value)
                if tok.is_dma and deps:
                    pv = prev_end.get(tok.q.name, 0)
                    if pv > 0:
                        need[tok.q.name] = max(need.get(tok.q.name, 0), pv)
                for nme, v in need.items():
                    if maxw.get(nme, -1) >= v:
                        continue
                    eng.wait_ge(sems[nme], v)
                    maxw[nme] = v
                inst = fn(eng)
                if tok.used:
                    assert inst is not None, f"token-op returned None ({q.name} #{tok.idx})"
                    inst.then_inc(sems[tok.q.name], tok.q.step)
                if tok.is_dma:
                    prev_end[tok.q.name] = tok.value

        block.tensor(lambda eng: run(self.pe, eng))
        block.scalar(lambda eng: run(self.sc, eng))
        block.vector(lambda eng: run(self.ve, eng))
        block.sync(lambda eng: run(self.sy, eng))
        block.gpsimd(lambda eng: run(self.gp, eng))



def build_program(T=2048, phases=("l0", "l0ffn", "l0attn", "l1", "l1attn", "l1ffn"),
                  debug=()):
    NT = T // 128
    NTC = T // 512
    nc = bass.Bass("TRN2", target_bir_lowering=False, debug=False)
    dram = lambda n, s, d, k="ExternalInput": nc.dram_tensor(n, s, d, kind=k).ap()
    x_h = dram("x", [T, D], F32)
    out_h = dram("out", [T, D], F32, "ExternalOutput")
    h = {}
    for n, s, d in [
        ("wq0", [D, D], F32), ("wk0", [D, D], F32), ("wv0", [D, D], F32), ("wo0", [D, D], F32),
        ("bqkv0", [128, 3, 8], F32), ("bo0", [128, 8], F32), ("lnw0", [1, D], F32), ("lnb0", [1, D], F32),
        ("fk0", [D, D], BF16), ("fs_in", [1, 1], F32), ("fs_k", [1, 1], F32), ("fb0", [128, 8], BF16),
        ("wq1", [D, D], F32), ("wk1", [256, D], F32), ("wv1", [256, D], F32), ("wo1", [D, D], F32),
        ("qA", [D, 32], F32), ("qB", [32, D], F32), ("kA", [D, 32], F32), ("kB", [32, 256], F32),
        ("vA", [D, 32], F32), ("vB", [32, 256], F32), ("pA", [D, 32], F32), ("pB", [32, D], F32),
        ("qgain", [128, 8], F32), ("hgate", [64, 16], F32),
        ("rW", [D, 8], F32), ("rb", [1, 8], F32),
        ("W1", [8 * 1024, 4096], F32), ("b1", [128, 256], F32),
        ("W2", [8 * 4096, 1024], F32), ("b2", [8, 1024], F32),
        ("ctab", [128, T], F32), ("s0tab", [128, T], F32), ("s1tab", [128, T], F32),
        ("cmask", [128, 2048], BF16), ("idf", [128, 128], F32), ("idb", [128, 128], BF16),
        ("vones", [128, 4 * 130], BF16), ("cvec", [128, 8], F32),
    ]:
        h[n] = dram(n, s, d)
    scr_xp = dram("scr_xp", [2, D, T], BF16, "Internal")
    scr_q = dram("scr_q", [2, 16, 128, T], BF16, "Internal")
    scr_k = dram("scr_k", [16, 128, T], BF16, "Internal")
    scr_v = dram("scr_v", [16, T, 130], BF16, "Internal")
    scr_r = dram("scr_r", [64, 512], F32, "Internal")
    scr_xa = dram("scr_xa", [2, 128, T], BF16, "Internal")
    dbg = {}
    for name in debug:
        shp, dty = _DBG_SHAPES(T)[name]
        dbg[name] = dram("dbg_" + name, shp, dty, "ExternalOutput")

    P = Prog()
    with ExitStack() as stk:
        sb = lambda n, shape, d=F32: stk.enter_context(nc.sbuf_tensor(n, shape, d))
        X = sb("X", [128, NT * 1024])
        idf = sb("idf_s", [128, 128]); idb = sb("idb_s", [128, 128], BF16)
        cmlb = sb("cmlb_s", [128, 2048], BF16)
        bqkv0 = sb("bqkv0_s", [128, 24]); bo0 = sb("bo0_s", [128, 8])
        fb0 = sb("fb0_s", [128, 8], BF16)
        qgain = sb("qgain_s", [128, 8]); hgate = sb("hgate_s", [64, 16])
        rb_b = sb("rb_b", [128, 8]); b1_s = sb("b1_s", [128, 256])
        cvec = sb("cvec_s", [128, 8]); fb0f = sb("fb0f", [128, 8])
        sin_b = sb("sin_b", [128, 1]); sk_b = sb("sk_b", [128, 1])
        rsin = sb("rsin", [128, 1]); rsk = sb("rsk", [128, 1]); sprod = sb("sprod", [128, 1])
        vstage = sb("vstage", [128, 4 * 130], BF16)
        s16 = sb("s16", [128, 1], F16)
        pAp = sb("pAp", [128, 512], BF16)
        lrB = sb("lrB", [32, 512], BF16)
        regA = sb("regA", [128, 2048])
        regB = sb("regB", [128, 2048])
        regC = sb("regC", [128, 2048])
        stat = sb("stat", [128, 32])
        stg = sb("stg", [128, 4096], BF16)
        WP = sb("WP", [128, 16384], BF16)
        WP2 = sb("WP2", [128, 16384], BF16)
        HP = sb("HP", [128, 8192], BF16)
        wfs = sb("wfs", [128, 2048])
        xq8 = WP[:, :].bitcast(F8)[:, 0:8 * T]
        WPf = WP[:, :].bitcast(F32)
        ctab = WPf[:, 4096:4096 + T]
        stab = WPf[:, 4096 + T:4096 + 2 * T]
        lnw_b = WP2[:, :].bitcast(F32)[:, 0:1024]
        lnb_b = WP2[:, :].bitcast(F32)[:, 1024:2048]
        gts = sb("gts", [128, 16 * NT])
        pb = [stk.enter_context(nc.psum_tensor(f"pb{i}", [128, 512], F32)) for i in range(8)]
        pbb = [b[:, :].bitcast(BF16) for b in pb]
        sems = {n: stk.enter_context(nc.semaphore("sem_" + n))
                for n in (["pe", "sc", "ve", "sy", "gp"]
                          + [f"g{i}" for i in range(10)] + [f"s{i}" for i in range(8)])}
        gch = [P.chan(P.gp, f"g{i}") for i in range(10)]
        sch = [P.chan(P.sy, f"s{i}") for i in range(8)]

        t_x = P.sy.op(lambda s: s.dma_start(
            out=X[:, :].rearrange("p (n f) -> p n f", n=NT),
            in_=x_h[:, :].rearrange("(n p) f -> p n f", p=128)))
        for dst, src in [(ctab, h["ctab"]), (stab, h["s0tab"]),
                         (idf, h["idf"]), (idb, h["idb"]),
                         (bo0, h["bo0"]), (fb0, h["fb0"]), (qgain, h["qgain"]),
                         (hgate, h["hgate"]), (b1_s, h["b1"]), (cvec, h["cvec"]),
                         (vstage, h["vones"])]:
            P.sy.op(lambda s, dst=dst, src=src: s.dma_start(out=dst[:, :], in_=src[:, :]))
        P.sy.op(lambda s: s.dma_start(out=bqkv0[:, :].rearrange("p (a b) -> p a b", a=3),
                                      in_=h["bqkv0"][:, :, :]))
        for dst, src in [(lnw_b, h["lnw0"]), (lnb_b, h["lnb0"]), (rb_b, h["rb"]),
                         (sin_b, h["fs_in"]), (sk_b, h["fs_k"])]:
            P.sy.op(lambda s, dst=dst, src=src: s.dma_start(
                out=dst[:, :], in_=src[0:1, :].partition_broadcast(128)))
        cmask = cmlb
        g_load0 = P.barrier()
        P.sc.op(lambda s: s.activation(out=fb0f[:, :], in_=fb0[:, :], func=AF.Copy), after=[g_load0])
        g_load = P.barrier()
        t_rsin = P.ve.op(lambda v: v.reciprocal(rsin[:, :], sin_b[:, :]), after=[g_load])
        t_rsk = P.ve.op(lambda v: v.reciprocal(rsk[:, :], sk_b[:, :]), after=[g_load])
        t_sprod = P.ve.op(lambda v: v.tensor_mul(sprod[:, :], sin_b[:, :], sk_b[:, :]), after=[g_load])

        def layernorm_tile(xt, xn_out, sq_buf, scol, after, affine=None, war=()):
            war = list(war)
            t_sq = P.sc.op(lambda s: s.activation(out=sq_buf, in_=xt, func=AF.Square),
                           after=list(after) + war)
            t_s1 = P.ve.op(lambda v: v.reduce_sum(stat[:, scol:scol + 1], xt, axis=AX), after=after)
            t_mu = P.ve.op(lambda v: v.tensor_scalar(stat[:, scol + 1:scol + 2], stat[:, scol:scol + 1],
                                                     1.0 / D, None, OP.mult), after=[t_s1])
            t_s2 = P.ve.op(lambda v: v.reduce_sum(stat[:, scol + 2:scol + 3], sq_buf, axis=AX), after=[t_sq])
            t_m2 = P.ve.op(lambda v: v.tensor_mul(stat[:, scol + 3:scol + 4], stat[:, scol + 1:scol + 2],
                                                  stat[:, scol + 1:scol + 2]), after=[t_mu])
            t_var = P.ve.op(lambda v: v.scalar_tensor_tensor(
                out=stat[:, scol + 4:scol + 5], in0=stat[:, scol + 2:scol + 3], scalar=1.0 / D,
                in1=stat[:, scol + 3:scol + 4], op0=OP.mult, op1=OP.subtract), after=[t_s2, t_m2])
            t_sd = P.sc.op(lambda s: s.activation(out=stat[:, scol + 5:scol + 6], in_=stat[:, scol + 4:scol + 5],
                                                  func=AF.Sqrt, bias=cvec[:, 1:2], scale=cvec[:, 3:4]),
                           after=[t_var])
            t_r = P.ve.op(lambda v: v.reciprocal(stat[:, scol + 4:scol + 5], stat[:, scol + 5:scol + 6]),
                          after=[t_sd])
            if affine is None:
                t_fin = P.ve.op(lambda v: v.tensor_scalar(xn_out, xt, stat[:, scol + 1:scol + 2],
                                                          stat[:, scol + 4:scol + 5], OP.subtract, OP.mult),
                                after=[t_r] + war)
            else:
                w_b, b_b = affine
                t_c = P.ve.op(lambda v: v.tensor_scalar(xn_out, xt, stat[:, scol + 1:scol + 2],
                                                        None, OP.subtract), after=[t_r] + war)
                t_w = P.ve.op(lambda v: v.tensor_mul(xn_out, xn_out, w_b), after=[t_c])
                t_rm = P.ve.op(lambda v: v.tensor_scalar(xn_out, xn_out, stat[:, scol + 4:scol + 5],
                                                         None, OP.mult), after=[t_w])
                t_fin = P.ve.op(lambda v: v.tensor_add(xn_out, xn_out, b_b), after=[t_rm])
            return t_fin

        if "l0" in phases:
            bank_free = [None] * 4
            sgl_free = [None] * 4
            hilo_free = [[] for _ in range(4)]
            tile_tp_done = {}
            tile_rd_done = {}
            for t in range(NT):
                xt = X[:, 1024 * t:1024 * (t + 1)]
                xn = regA[:, 1024 * (t % 2):1024 * (t % 2) + 1024]
                x2 = regB[:, 1024 * (t % 2):1024 * (t % 2) + 1024]
                sq = regC[:, 1024 * (t % 2):1024 * (t % 2) + 1024]
                war = []
                if t >= 2:
                    war = [tile_tp_done[t - 2], tile_rd_done[t - 2]]
                t_xn = layernorm_tile(xt, xn, sq, 8 * (t % 2), [g_load], war=war)
                t_acc = P.ve.op(lambda v, xt=xt, xn=xn: v.tensor_add(xt, xt, xn), after=[t_xn])
                t_x2 = layernorm_tile(xn, x2, sq, 8 * (t % 2), [t_xn], affine=(lnw_b, lnb_b))
                last_q8 = None
                for c in range(8):
                    bk, bkn = pb[c % 4], c % 4
                    dep = [t_acc, t_x2]
                    if bank_free[bkn] is not None:
                        dep.append(bank_free[bkn])
                    t_tp1 = P.pe.op(lambda p, bk=bk, xn=xn, c=c: p.transpose(
                        bk[:, 0:128], xn[:, 128 * c:128 * (c + 1)], idf[:, :]), after=dep)
                    t_tp2 = P.pe.op(lambda p, bk=bk, x2=x2, c=c: p.transpose(
                        bk[:, 128:256], x2[:, 128 * c:128 * (c + 1)], idf[:, :]), after=[t_tp1])
                    sgl = stg[:, 512 * (c % 4):512 * (c % 4) + 128]
                    hi = stg[:, 2048 + 512 * (c % 4):2048 + 512 * (c % 4) + 128]
                    lo = stg[:, 2048 + 512 * (c % 4) + 128:2048 + 512 * (c % 4) + 256]
                    t_e1 = P.sc.op(lambda s, bk=bk, sgl=sgl: s.activation(
                        out=sgl, in_=bk[:, 0:128], func=AF.Copy),
                        after=[t_tp2] + ([sgl_free[c % 4]] if sgl_free[c % 4] else []))
                    t_q8 = P.ve.op(lambda v, sgl=sgl, t=t, c=c: v.tensor_scalar(
                        xq8[:, T * c + 128 * t:T * c + 128 * (t + 1)], sgl, rsin[:, 0:1], None, OP.mult),
                        after=[t_e1, t_rsin])
                    t_hi = P.sc.op(lambda s, bk=bk, hi=hi: s.activation(
                        out=hi, in_=bk[:, 128:256], func=AF.Copy),
                        after=[t_tp2] + hilo_free[c % 4])
                    t_lo = P.ve.op(lambda v, bk=bk, hi=hi, lo=lo: v.tensor_sub(lo, bk[:, 128:256], hi),
                                   after=[t_hi])
                    bank_free[bkn] = t_lo
                    sgl_free[c % 4] = t_q8
                    t_d1 = gch[c % 4].op(lambda g, hi=hi, t=t, c=c: g.dma_start(
                        out=scr_xp[0, 128 * c:128 * (c + 1), 128 * t:128 * (t + 1)], in_=hi), after=[t_hi])
                    t_d2 = gch[c % 4].op(lambda g, lo=lo, t=t, c=c: g.dma_start(
                        out=scr_xp[1, 128 * c:128 * (c + 1), 128 * t:128 * (t + 1)], in_=lo), after=[t_lo])
                    hilo_free[c % 4] = [t_d1, t_d2]
                    last_q8 = t_q8
                tile_tp_done[t] = t_tp2
                tile_rd_done[t] = t_lo
            g_l0 = P.barrier()
        else:
            g_l0 = g_load

        if "l0ffn" in phases:
            kq = HP[:, :].bitcast(F8)[:, 0:2048]
            ev_done = {}
            add_done = {}
            t_kq_prev = None
            for c in range(8):
                sl = 1024 * (c % 2)
                dep_l = [g_l0]
                if c >= 2:
                    dep_l.append(ev_done[(c - 2) * NTC + NTC - 1])
                t_kl = sch[c % 2].op(lambda s, c=c, sl=sl: s.dma_start(
                    out=stg[:, sl:sl + 1024].rearrange("p (a f) -> p a f", a=8),
                    in_=h["fk0"][:, 128 * c:128 * (c + 1)].rearrange("(a p) f -> p a f", p=128)),
                    after=dep_l)
                dep_q = [t_kl, t_rsk]
                if c >= 2:
                    dep_q.append(ev_done[(c - 2) * NTC + NTC - 1])
                t_kq = P.ve.op(lambda v, c=c, sl=sl: v.tensor_scalar(
                    kq[:, sl:sl + 1024], stg[:, sl:sl + 1024], rsk[:, 0:1], None, OP.mult), after=dep_q)
                for tc in range(NTC):
                    i = c * NTC + tc
                    bk = pb[4 + i % 2]
                    dep0 = [t_kq, g_l0]
                    if i >= 2:
                        dep0.append(ev_done[i - 2])
                    for k in range(8):
                        t_mm = P.pe.op(lambda p, bk=bk, k=k, sl=sl, tc=tc: p.matmul(
                            bk[:, :], kq[:, sl + 128 * k:sl + 128 * (k + 1)],
                            xq8[:, T * k + 512 * tc:T * k + 512 * (tc + 1)],
                            start=(k == 0), stop=(k == 7)), after=dep0 if k == 0 else [])
                    ybf = stg[:, 2048 + 1024 * (i % 2):2048 + 1024 * (i % 2) + 512]
                    dep_ev = [t_mm, t_sprod]
                    if i >= 2:
                        dep_ev.append(add_done[i - 2])
                    t_ev = P.sc.op(lambda s, bk=bk, ybf=ybf: s.activation(
                        out=ybf, in_=bk[:, :], func=AF.Copy, scale=sprod[:, 0:1]), after=dep_ev)
                    t_b = P.ve.op(lambda v, ybf=ybf, c=c: v.tensor_scalar(
                        ybf, ybf, fb0f[:, c:c + 1], None, OP.add), after=[t_ev])
                    bkb = pbb[6]
                    tps = []
                    for b4 in range(4):
                        dep2 = [t_b] + ([add_done[i - 1]] if i >= 1 else [])
                        tps.append(P.pe.op(lambda p, bkb=bkb, ybf=ybf, b4=b4: p.transpose(
                            bkb[:, 128 * b4:128 * (b4 + 1)], ybf[:, 128 * b4:128 * (b4 + 1)], idb[:, :]),
                            after=dep2 if b4 == 0 else []))
                    t_add = P.ve.op(lambda v, bkb=bkb, c=c, tc=tc: v.tensor_add(
                        X[:, :].rearrange("p (n f) -> p n f", n=NT)[:, 4 * tc:4 * tc + 4, 128 * c:128 * (c + 1)],
                        X[:, :].rearrange("p (n f) -> p n f", n=NT)[:, 4 * tc:4 * tc + 4, 128 * c:128 * (c + 1)],
                        bkb[:, 0:512].rearrange("p (a f) -> p a f", a=4)), after=[tps[-1]])
                    ev_done[i] = t_ev
                    add_done[i] = t_add
            g_ffn0 = P.barrier()
        else:
            g_ffn0 = g_l0

        def attn_qkv_phase(layer, g_prev):
            stgf = stg[:, :].bitcast(F32)
            kst_war = [None]
            vstage_rd = [None]
            vfree = [None, None, None, None]
            vev_prev = [None, None]
            wslot = [None, None]
            xslot = [None, None]
            scr_done = []
            pb_free = [None] * 8
            for g in range(8):
                raws = {}
                for pi, (wname, raw) in enumerate([("wq0", regA), ("wk0", regB), ("wv0", regC)]):
                    ws = pi % 2 if g % 2 == 0 else (pi + 1) % 2
                    t_wl = sch[ws].op(lambda s, wname=wname, g=g, ws=ws: s.dma_start(
                        out=wfs[:, 1024 * ws:1024 * (ws + 1)].rearrange("p (a f) -> p a f", a=8),
                        in_=h[wname][:, 128 * g:128 * (g + 1)].rearrange("(a p) f -> p a f", p=128)),
                        after=[g_prev] + ([wslot[ws]] if wslot[ws] else []))
                    hi_ap = HP[:, 2048 * pi:2048 * (pi + 1)].rearrange("p (k two f) -> p k two f", k=8, two=2)[:, :, 0, :]
                    lo_ap = HP[:, 2048 * pi:2048 * (pi + 1)].rearrange("p (k two f) -> p k two f", k=8, two=2)[:, :, 1, :]
                    t_wh = P.sc.op(lambda s, ws=ws, hi_ap=hi_ap: s.activation(
                        out=hi_ap, in_=wfs[:, 1024 * ws:1024 * (ws + 1)].rearrange("p (a f) -> p a f", a=8),
                        func=AF.Copy), after=[t_wl])
                    t_wlo = P.ve.op(lambda v, ws=ws, hi_ap=hi_ap, lo_ap=lo_ap: v.tensor_tensor(
                        lo_ap, wfs[:, 1024 * ws:1024 * (ws + 1)].rearrange("p (a f) -> p a f", a=8),
                        hi_ap, OP.subtract), after=[t_wh])
                    wslot[ws] = t_wlo
                    raws[pi] = (raw, t_wh, t_wlo)
                ev_toks = {0: [], 1: [], 2: []}
                for tc in range(NTC):
                    banks = [pb[(0 if tc % 2 == 0 else 3) + i] for i in range(3)]
                    for k in range(8):
                        xs = k % 2
                        xhi = HP[:, 6144 + 1024 * xs:6144 + 1024 * xs + 512]
                        xlo = HP[:, 6144 + 1024 * xs + 512:6144 + 1024 * xs + 1024]
                        t_x1 = sch[2 + xs].op(lambda s, xhi=xhi, k=k, tc=tc: s.dma_start(
                            out=xhi, in_=scr_xp[0, 128 * k:128 * (k + 1), 512 * tc:512 * (tc + 1)]),
                            after=[g_prev] + ([xslot[xs]] if xslot[xs] else []))
                        t_x2 = sch[2 + xs].op(lambda s, xlo=xlo, k=k, tc=tc: s.dma_start(
                            out=xlo, in_=scr_xp[1, 128 * k:128 * (k + 1), 512 * tc:512 * (tc + 1)]))
                        last_mm = None
                        for pi in range(3):
                            raw, t_wh, t_wlo = raws[pi]
                            wsl = HP[:, 2048 * pi:2048 * (pi + 1)]
                            whi = wsl[:, 256 * k:256 * k + 128]
                            wlo = wsl[:, 256 * k + 128:256 * k + 256]
                            dep0 = [t_x1, t_x2, t_wh, t_wlo]
                            if k == 0 and pb_free[(0 if tc % 2 == 0 else 3) + pi] is not None:
                                dep0.append(pb_free[(0 if tc % 2 == 0 else 3) + pi])
                            for psn, (wop, xop) in enumerate([(whi, xhi), (wlo, xhi), (whi, xlo)]):
                                last_mm = P.pe.op(lambda p, b=banks[pi], wop=wop, xop=xop,
                                                  st=(k == 0 and psn == 0), sp=(k == 7 and psn == 2):
                                                  p.matmul(b[:, :], wop, xop, start=st, stop=sp),
                                                  after=dep0 if psn == 0 else [])
                            ev_toks[pi].append(last_mm)
                        xslot[xs] = last_mm
                    for pi in range(3):
                        raw = raws[pi][0]
                        t_ev = P.sc.op(lambda s, b=banks[pi], raw=raw, tc=tc, pi=pi, g=g: s.activation(
                            out=raw[:, 512 * tc:512 * (tc + 1)], in_=b[:, :], func=AF.Identity,
                            bias=bqkv0[:, 8 * pi + g:8 * pi + g + 1], scale=cvec[:, 3:4]), after=[ev_toks[pi][-1]])
                        pb_free[(0 if tc % 2 == 0 else 3) + pi] = t_ev
                        ev_toks[pi].append(t_ev)
                t_q_ev = [t for t in ev_toks[0] ]
                def rope(raw, evs):
                    t_sh = P.ve.op(lambda v, raw=raw: v.stream_shuffle(stgf[:, 0:T], raw[:, 0:T], _SWAP16),
                                   after=evs)
                    t_s = P.ve.op(lambda v: v.tensor_mul(stgf[:, 0:T], stab[:, 0:T], stgf[:, 0:T]), after=[t_sh])
                    t_c = P.ve.op(lambda v, raw=raw: v.tensor_mul(raw[:, 0:T], ctab[:, 0:T], raw[:, 0:T]),
                                  after=[t_sh])
                    return P.ve.op(lambda v, raw=raw: v.tensor_add(raw[:, 0:T], raw[:, 0:T], stgf[:, 0:T]),
                                   after=[t_s, t_c])
                t_qr = rope(regA, [ev_toks[0][-1]])
                t_kr = rope(regB, [ev_toks[1][-1], t_qr])
                wr = []
                q_wr = []
                for hh in range(2):
                    hd = g * 2 + hh
                    rs = slice(64 * hh, 64 * hh + 64)
                    qh = stg[rs, 0:2048][:, 0:T]
                    t_h1 = P.sc.op(lambda s, rs=rs: s.activation(
                        out=stg[rs, 0:T], in_=regA[rs, 0:T], func=AF.Copy),
                        after=[t_qr, t_kr] + ([vstage_rd[0]] if vstage_rd[0] else []))
                    t_l1 = P.ve.op(lambda v, rs=rs: v.tensor_sub(
                        stg[rs, T:2 * T], regA[rs, 0:T], stg[rs, 0:T]), after=[t_h1])
                    for half in range(2):
                        q_wr.append(gch[0].op(lambda g_, hd=hd, rs=rs, half=half: g_.dma_start(
                            out=scr_q[0, hd, 64 * half:64 * half + 64, :], in_=stg[rs, 0:T]), after=[t_h1]))
                        q_wr.append(gch[0].op(lambda g_, hd=hd, rs=rs, half=half: g_.dma_start(
                            out=scr_q[1, hd, 64 * half:64 * half + 64, :], in_=stg[rs, T:2 * T]), after=[t_l1]))
                    wr += q_wr[-4:]
                    t_h2 = P.sc.op(lambda s, rs=rs: s.activation(
                        out=WP[rs, 0:T], in_=regB[rs, 0:T], func=AF.Copy), after=[t_kr, kst_war[0]] if kst_war[0] else [t_kr])
                    t_l2 = P.ve.op(lambda v, rs=rs: v.tensor_sub(
                        WP[rs, T:2 * T], regB[rs, 0:T], WP[rs, 0:T]), after=[t_h2])
                    wr.append(gch[4].op(lambda g_, hd=hd, rs=rs: g_.dma_start(
                        out=scr_k[hd, 0:64, :], in_=WP[rs, 0:T]), after=[t_h2]))
                    wr.append(gch[4].op(lambda g_, hd=hd, rs=rs: g_.dma_start(
                        out=scr_k[hd, 64:128, :], in_=WP[rs, T:2 * T]), after=[t_l2]))
                    kst_war[0] = wr[-1]
                t_vh = P.sc.op(lambda s: s.activation(out=stg[:, 0:T], in_=regC[:, 0:T], func=AF.Copy),
                               after=[ev_toks[2][-1]] + q_wr)
                t_vl = P.ve.op(lambda v: v.tensor_sub(stg[:, T:2 * T], regC[:, 0:T], stg[:, 0:T]),
                               after=[t_vh] + q_wr)
                for hh in range(2):
                    hd = g * 2 + hh
                    rs = slice(64 * hh, 64 * hh + 64)
                    idsl = idb[rs, rs]
                    for kt in range(NT):
                        sl = (hh * NT + kt) % 4
                        b6, b7 = pbb[6], pbb[7]
                        dep = [t_vh, t_vl] + ([vfree[sl]] if vfree[sl] else [])
                        if vev_prev[0] is not None:
                            dep += vev_prev
                        t_t1 = P.pe.op(lambda p, rs=rs, kt=kt, idsl=idsl, sl=sl, b6=b6: p.transpose(
                            b6[:, 128 * sl:128 * sl + 64], stg[rs, 128 * kt:128 * (kt + 1)], idsl),
                            after=dep)
                        t_t2 = P.pe.op(lambda p, rs=rs, kt=kt, idsl=idsl, sl=sl, b7=b7: p.transpose(
                            b7[:, 128 * sl:128 * sl + 64], stg[rs, T + 128 * kt:T + 128 * (kt + 1)], idsl),
                            after=[t_t1])
                        t_e1 = P.sc.op(lambda s, sl=sl, b6=b6: s.activation(
                            out=vstage[:, 130 * sl:130 * sl + 64], in_=b6[:, 128 * sl:128 * sl + 64],
                            func=AF.Copy), after=[t_t1] + ([vfree[sl]] if vfree[sl] else []))
                        t_e2 = P.sc.op(lambda s, sl=sl, b7=b7: s.activation(
                            out=vstage[:, 130 * sl + 65:130 * sl + 129], in_=b7[:, 128 * sl:128 * sl + 64],
                            func=AF.Copy), after=[t_t2] + ([vfree[sl]] if vfree[sl] else []))
                        t_vw = gch[6 + sl].op(lambda g_, hd=hd, kt=kt, sl=sl: g_.dma_start(
                            out=scr_v[hd, 128 * kt:128 * (kt + 1), :], in_=vstage[:, 130 * sl:130 * (sl + 1)]),
                            after=[t_e1, t_e2])
                        vfree[sl] = t_vw
                        wr.append(t_vw)
                        vstage_rd[0] = t_t2
                        vev_prev[0], vev_prev[1] = t_e1, t_e2
                scr_done += wr
            return P.barrier()

        def attn_core_phase(layer, g_prev):
            rzv = regB[:, :]
            avdep = [None, None]
            stdep = [None, None]
            sbi = [0]
            qsl_dep = [None, None]
            ksl_dep = [None, None]
            esl_dep = [None, None]
            avst_dep = [None, None]
            rr_dep = [None] * 4
            for hd in range(16):
                khd = hd if layer == 0 else hd // 4
                ks = hd % 2
                t_kld = sch[6 + ks].op(lambda s, khd=khd, ks=ks: s.dma_start(
                    out=stg[:, 0:T] if ks == 0 else stg[:, 2048:2048 + T], in_=scr_k[khd, :, :]),
                    after=[g_prev] + ([ksl_dep[ks]] if ksl_dep[ks] else []))
                kbuf = stg[:, 0:T] if ks == 0 else stg[:, 2048:2048 + T]
                t_vld = sch[6 + ks].op(lambda s, khd=khd, ks=ks: s.dma_start(
                    out=HP[:, 2080 * ks:2080 * ks + 130 * NT].rearrange("p (a f) -> p a f", a=NT),
                    in_=scr_v[khd, :, :].rearrange("(a p) f -> p a f", p=128)))
                vbuf = HP[:, 2080 * ks:2080 * ks + 130 * NT]
                k_last_use = None
                for qc in range(NTC):
                    kts = list(range(NT)) if layer == 0 else list(range(min(NT, 4 * qc + 4)))
                    qs = qc % 2
                    qhi = HP[:, 4160 + 1024 * qs:4160 + 1024 * qs + 512]
                    qlo = HP[:, 4160 + 1024 * qs + 512:4160 + 1024 * qs + 1024]
                    t_q1 = sch[4 + qs].op(lambda s, hd=hd, qc=qc, qhi=qhi: s.dma_start(
                        out=qhi, in_=scr_q[0, hd, :, 512 * qc:512 * (qc + 1)]),
                        after=[g_prev] + ([qsl_dep[qs]] if qsl_dep[qs] else []))
                    t_q2 = sch[4 + qs].op(lambda s, hd=hd, qc=qc, qlo=qlo: s.dma_start(
                        out=qlo, in_=scr_q[1, hd, :, 512 * qc:512 * (qc + 1)]))
                    avb = pb[2 + (hd * NTC + qc) % 2]
                    t_avmm = None
                    for kt in kts:
                        sbi[0] = 1 - sbi[0]
                        stb = pb[sbi[0]]
                        dep_s = [t_kld, t_q1, t_q2]
                        if stdep[sbi[0]] is not None:
                            dep_s.append(stdep[sbi[0]])
                        t_s1 = P.pe.op(lambda p, stb=stb, kbuf=kbuf, kt=kt, qhi=qhi: p.matmul(
                            stb[:, :], kbuf[:, 128 * kt:128 * (kt + 1)], qhi, start=True, stop=False),
                            after=dep_s)
                        t_s2 = P.pe.op(lambda p, stb=stb, kbuf=kbuf, kt=kt, qlo=qlo: p.matmul(
                            stb[:, :], kbuf[:, 128 * kt:128 * (kt + 1)], qlo, start=False, stop=True),
                            after=[])
                        k_last_use = t_s2
                        efr = regA[:, 512 * (kt % 2):512 * (kt % 2) + 512]
                        dep_e = [t_s2]
                        if esl_dep[kt % 2] is not None:
                            dep_e.append(esl_dep[kt % 2])
                        if layer == 1 and kt >= 4 * qc:
                            dmask = kt - 4 * qc
                            t_msk = P.ve.op(lambda v, stb=stb, efr=efr, dmask=dmask: v.scalar_tensor_tensor(
                                out=efr, in0=stb[:, :], scalar=0.125,
                                in1=cmlb[:, 512 * dmask:512 * (dmask + 1)],
                                op0=OP.mult, op1=OP.add), after=dep_e)
                            t_exp = P.sc.op(lambda s, efr=efr: s.activation(
                                out=efr, in_=efr, func=AF.Exp, scale=cvec[:, 3:4]), after=[t_msk])
                        else:
                            t_exp = P.sc.op(lambda s, stb=stb, efr=efr: s.activation(
                                out=efr, in_=stb[:, :], func=AF.Exp, scale=cvec[:, 2:3]), after=dep_e)
                        stdep[sbi[0]] = t_exp
                        ecv = regC[:, :].bitcast(BF16)
                        ehi = ecv[:, 1024 * (kt % 2):1024 * (kt % 2) + 512]
                        elo = ecv[:, 1024 * (kt % 2) + 512:1024 * (kt % 2) + 1024]
                        t_eh = P.ve.op(lambda v, efr=efr, ehi=ehi: v.tensor_copy(ehi, efr), after=[t_exp])
                        t_el = P.ve.op(lambda v, efr=efr, ehi=ehi, elo=elo: v.tensor_sub(elo, efr, ehi),
                                       after=[t_eh])
                        dep_av = [t_eh, t_el, t_vld]
                        if kt == 0 and avdep[(hd * NTC + qc) % 2] is not None:
                            dep_av.append(avdep[(hd * NTC + qc) % 2])
                        vh_ap = vbuf[:, 130 * kt:130 * kt + 65]
                        vl_ap = vbuf[:, 130 * kt + 65:130 * kt + 130]
                        for psn, (vop, eop) in enumerate([(vh_ap, ehi), (vh_ap, elo), (vl_ap, ehi)]):
                            t_avmm = P.pe.op(lambda p, avb=avb, vop=vop, eop=eop,
                                             st=(kt == 0 and psn == 0), sp=(kt == NT - 1 and psn == 2):
                                             p.matmul(avb[0:65, :], vop, eop, start=st, stop=sp),
                                             after=dep_av if psn == 0 else [])
                        esl_dep[kt % 2] = t_avmm
                    qsl_dep[qs] = t_avmm
                    rr = hd * NTC + qc
                    t_rz = P.ve.op(lambda v, avb=avb: v.reciprocal(rzv[0:1, 0:512], avb[64:65, 0:512]),
                                   after=[t_avmm] + ([rr_dep[rr % 4]] if rr_dep[rr % 4] else []))
                    t_rw = gch[rr % 4].op(lambda g_, rr=rr: g_.dma_start(
                        out=scr_r[rr:rr + 1, :], in_=rzv[0:1, 0:512]), after=[t_rz])
                    t_rb = gch[rr % 4].op(lambda g_, rr=rr: g_.dma_start(
                        out=rzv[0:64, 512:1024], in_=scr_r[rr:rr + 1, :].partition_broadcast(64)),
                        after=[t_rw])
                    if layer == 0:
                        t_sc1 = P.ve.op(lambda v, avb=avb: v.tensor_tensor(
                            rzv[0:64, 1024:1536], avb[0:64, 0:512], rzv[0:64, 512:1024], OP.mult),
                            after=[t_rb])
                    else:
                        t_sc1 = P.ve.op(lambda v, avb=avb, hd=hd: v.scalar_tensor_tensor(
                            out=rzv[0:64, 1024:1536], in0=avb[0:64, 0:512],
                            scalar=hgate[0:64, hd:hd + 1], in1=rzv[0:64, 512:1024],
                            op0=OP.mult, op1=OP.mult), after=[t_rb])
                    rr_dep[rr % 4] = t_sc1
                    avdep[(hd * NTC + qc) % 2] = t_sc1
                    regBb = regB[:, :].bitcast(BF16)
                    dep_a = [t_sc1] + ([avst_dep[0]] if avst_dep[0] else [])
                    t_ah = P.sc.op(lambda s, regBb=regBb: s.activation(
                        out=regBb[0:64, 3072:3584],
                        in_=rzv[0:64, 1024:1536], func=AF.Copy), after=dep_a)
                    t_al = P.ve.op(lambda v, regBb=regBb: v.tensor_sub(
                        regBb[0:64, 3584:4096],
                        rzv[0:64, 1024:1536], regBb[0:64, 3072:3584]),
                        after=[t_ah])
                    wcol = T * (hd // 2) + 512 * qc
                    prow = slice(64 * (hd % 2), 64 * (hd % 2) + 64)
                    t_w1 = gch[4].op(lambda g_, regBb=regBb, wcol=wcol, prow=prow: g_.dma_start(
                        out=WP[prow, wcol:wcol + 512],
                        in_=regBb[0:64, 3072:3584]), after=[t_ah])
                    t_w2 = gch[4].op(lambda g_, regBb=regBb, wcol=wcol, prow=prow: g_.dma_start(
                        out=WP2[prow, wcol:wcol + 512],
                        in_=regBb[0:64, 3584:4096]), after=[t_al])
                    avst_dep[0] = t_w2
                ksl_dep[ks] = k_last_use
            return P.barrier()

        def attn_out_phase(layer, g_prev):
            wslot2 = [None, None]
            bdep = [None, None]
            xadd = None
            for c in range(8):
                ws = c % 2
                t_wl = sch[ws].op(lambda s, c=c, ws=ws: s.dma_start(
                    out=wfs[:, 1024 * ws:1024 * (ws + 1)].rearrange("p (a f) -> p a f", a=8),
                    in_=h["wo0"][:, 128 * c:128 * (c + 1)].rearrange("(a p) f -> p a f", p=128)),
                    after=[g_prev] + ([wslot2[ws]] if wslot2[ws] else []))
                hi_ap = HP[:, 2048 * ws:2048 * (ws + 1)].rearrange("p (k two f) -> p k two f", k=8, two=2)[:, :, 0, :]
                lo_ap = HP[:, 2048 * ws:2048 * (ws + 1)].rearrange("p (k two f) -> p k two f", k=8, two=2)[:, :, 1, :]
                t_wh = P.sc.op(lambda s, ws=ws, hi_ap=hi_ap: s.activation(
                    out=hi_ap, in_=wfs[:, 1024 * ws:1024 * (ws + 1)].rearrange("p (a f) -> p a f", a=8),
                    func=AF.Copy), after=[t_wl])
                t_wlo = P.ve.op(lambda v, ws=ws, hi_ap=hi_ap, lo_ap=lo_ap: v.tensor_tensor(
                    lo_ap, wfs[:, 1024 * ws:1024 * (ws + 1)].rearrange("p (a f) -> p a f", a=8),
                    hi_ap, OP.subtract), after=[t_wh])
                wslot2[ws] = t_wlo
                for tc in range(NTC):
                    i = c * NTC + tc
                    bkk = pb[i % 2]
                    dep0 = [t_wh, t_wlo, g_prev]
                    if bdep[i % 2] is not None:
                        dep0.append(bdep[i % 2])
                    t_mm = None
                    for k in range(8):
                        whi = HP[:, 2048 * ws + 256 * k:2048 * ws + 256 * k + 128]
                        wlo = HP[:, 2048 * ws + 256 * k + 128:2048 * ws + 256 * k + 256]
                        for psn, (wop, xop) in enumerate([
                                (whi, WP[:, T * k + 512 * tc:T * k + 512 * (tc + 1)]),
                                (wlo, WP[:, T * k + 512 * tc:T * k + 512 * (tc + 1)]),
                                (whi, WP2[:, T * k + 512 * tc:T * k + 512 * (tc + 1)])]):
                            t_mm = P.pe.op(lambda p, bkk=bkk, wop=wop, xop=xop,
                                           st=(k == 0 and psn == 0), sp=(k == 7 and psn == 2):
                                           p.matmul(bkk[:, :], wop, xop, start=st, stop=sp),
                                           after=dep0 if (k == 0 and psn == 0) else [])
                    ofr = regA[:, 512 * (i % 2):512 * (i % 2) + 512]
                    if layer == 0:
                        t_ev = P.sc.op(lambda s, bkk=bkk, ofr=ofr, c=c: s.activation(
                            out=ofr, in_=bkk[:, :], func=AF.Identity, bias=bo0[:, c:c + 1],
                            scale=cvec[:, 3:4]), after=[t_mm])
                    else:
                        t_ev = P.sc.op(lambda s, bkk=bkk, ofr=ofr: s.activation(
                            out=ofr, in_=bkk[:, :], func=AF.Copy), after=[t_mm])
                    bdep[i % 2] = t_ev
                    tb = pb[4 + i % 2]
                    tps = None
                    for b4 in range(4):
                        dep2 = [t_ev] + ([xadd] if xadd is not None and b4 == 0 else [])
                        tps = P.pe.op(lambda p, tb=tb, ofr=ofr, b4=b4: p.transpose(
                            tb[:, 128 * b4:128 * (b4 + 1)], ofr[:, 128 * b4:128 * (b4 + 1)], idf[:, :]),
                            after=dep2 if b4 == 0 else [])
                    xadd = P.ve.op(lambda v, tb=tb, c=c, tc=tc: v.tensor_add(
                        X[:, :].rearrange("p (n f) -> p n f", n=NT)[:, 4 * tc:4 * tc + 4, 128 * c:128 * (c + 1)],
                        X[:, :].rearrange("p (n f) -> p n f", n=NT)[:, 4 * tc:4 * tc + 4, 128 * c:128 * (c + 1)],
                        tb[:, 0:512].rearrange("p (a f) -> p a f", a=4)), after=[tps])
            return P.barrier()

        if "l0attn" in phases:
            g_3a = attn_qkv_phase(0, g_ffn0)
            g_3b = attn_core_phase(0, g_3a)
            g_ffn0 = attn_out_phase(0, g_3b)

        if "l1" in phases:
            g_prev = g_ffn0
            t_s1 = P.sy.op(lambda s: s.dma_start(out=stab[:, :], in_=h["s1tab"][:, :]), after=[g_prev])
            t_ct2 = P.sy.op(lambda s: s.dma_start(out=ctab[:, :], in_=h["ctab"][:, :]), after=[t_s1])
            t_cm = P.sy.op(lambda s: s.dma_start(out=cmlb[:, :], in_=h["cmask"][:, :]), after=[t_ct2])
            rwp = HP[:, 0:128]
            t_rl = P.sy.op(lambda s: s.dma_start(
                out=wfs[:, 0:64].rearrange("p (a e) -> p a e", a=8),
                in_=h["rW"][:, :].rearrange("(a p) e -> p a e", p=128)), after=[t_cm])
            rw_hi = rwp.rearrange("p (k two e) -> p k two e", k=8, two=2)[:, :, 0, :]
            rw_lo = rwp.rearrange("p (k two e) -> p k two e", k=8, two=2)[:, :, 1, :]
            t_rh = P.sc.op(lambda s: s.activation(
                out=rw_hi, in_=wfs[:, 0:64].rearrange("p (a e) -> p a e", a=8), func=AF.Copy),
                after=[t_rl])
            t_rlo = P.ve.op(lambda v: v.tensor_tensor(
                rw_lo, wfs[:, 0:64].rearrange("p (a e) -> p a e", a=8), rw_hi, OP.subtract),
                after=[t_rh])
            bank_free = [None] * 4
            sgl_free = [None] * 4
            hilo_free = [[] for _ in range(4)]
            tile_tp_done = {}
            tile_rd_done = {}
            t_rmm_last = {}
            rb_free = {}
            for t in range(NT):
                xt = X[:, 1024 * t:1024 * (t + 1)]
                xn = regA[:, 1024 * (t % 2):1024 * (t % 2) + 1024]
                sq = regB[:, 1024 * (t % 2):1024 * (t % 2) + 1024]
                war = []
                if t >= 2:
                    war = [tile_tp_done[t - 2], tile_rd_done[t - 2]]
                t_xn = layernorm_tile(xt, xn, sq, 8 * (t % 2), [g_prev], war=war)
                rb_bank = pb[4 + t % 2]
                t_rmm = None
                rb_war = rb_free.get(t - 2)
                for c in range(8):
                    bk, bkn = pb[c % 4], c % 4
                    dep = [t_xn]
                    if bank_free[bkn] is not None:
                        dep.append(bank_free[bkn])
                    t_tp = P.pe.op(lambda p, bk=bk, xn=xn, c=c: p.transpose(
                        bk[:, 0:128], xn[:, 128 * c:128 * (c + 1)], idf[:, :]), after=dep)
                    hi = stg[:, 512 * (c % 4):512 * (c % 4) + 128]
                    lo = stg[:, 512 * (c % 4) + 128:512 * (c % 4) + 256]
                    t_hi = P.sc.op(lambda s, bk=bk, hi=hi: s.activation(
                        out=hi, in_=bk[:, 0:128], func=AF.Copy),
                        after=[t_tp] + hilo_free[c % 4] + ([sgl_free[c % 4]] if sgl_free[c % 4] else []))
                    t_lo = P.ve.op(lambda v, bk=bk, hi=hi, lo=lo: v.tensor_sub(lo, bk[:, 0:128], hi),
                                   after=[t_hi])
                    bank_free[bkn] = t_lo
                    t_d1 = gch[c % 4].op(lambda g_, hi=hi, t=t, c=c: g_.dma_start(
                        out=scr_xp[0, 128 * c:128 * (c + 1), 128 * t:128 * (t + 1)], in_=hi), after=[t_hi])
                    t_d2 = gch[c % 4].op(lambda g_, lo=lo, t=t, c=c: g_.dma_start(
                        out=scr_xp[1, 128 * c:128 * (c + 1), 128 * t:128 * (t + 1)], in_=lo), after=[t_lo])
                    hilo_free[c % 4] = [t_d1, t_d2]
                    dep_r = [t_hi, t_lo, t_rh, t_rlo]
                    if c == 0 and rb_war is not None:
                        dep_r.append(rb_war)
                    for psn, (xop, rop) in enumerate([
                            (hi, rwp[:, 16 * c:16 * c + 8]),
                            (lo, rwp[:, 16 * c:16 * c + 8]),
                            (hi, rwp[:, 16 * c + 8:16 * c + 16])]):
                        t_rmm = P.pe.op(lambda p, rb_bank=rb_bank, xop=xop, rop=rop,
                                        st=(c == 0 and psn == 0), sp=(c == 7 and psn == 2):
                                        p.matmul(rb_bank[:, 0:8], xop, rop, start=st, stop=sp),
                                        after=dep_r if psn == 0 else [])
                    sgl_free[c % 4] = t_rmm
                t_rmm_last[t] = t_rmm
                tile_tp_done[t] = t_tp
                lg = gts[:, 16 * t:16 * t + 8]
                gt_ = gts[:, 16 * t + 8:16 * t + 16]
                sc_ = 16 + 8 * (t % 2)
                dep_ev = [t_rmm]
                if t >= 2 and (t - 2) in t_rmm_last:
                    pass
                t_lg = P.ve.op(lambda v, lg=lg, rb_bank=rb_bank: v.tensor_add(
                    lg, rb_bank[:, 0:8], rb_b[:, :]), after=dep_ev)
                rb_free[t] = t_lg
                t_mx = P.ve.op(lambda v, lg=lg, sc_=sc_: v.reduce_max(
                    stat[:, sc_:sc_ + 1], lg, axis=AX, negate=True), after=[t_lg])
                t_pb = P.sc.op(lambda s, lg=lg, gt_=gt_, sc_=sc_: s.activation(
                    out=gt_, in_=lg, func=AF.Exp, bias=stat[:, sc_:sc_ + 1], scale=cvec[:, 3:4]),
                    after=[t_mx])
                t_sm = P.ve.op(lambda v, gt_=gt_, sc_=sc_: v.reduce_sum(
                    stat[:, sc_ + 1:sc_ + 2], gt_, axis=AX), after=[t_pb])
                t_rc = P.ve.op(lambda v, sc_=sc_: v.reciprocal(
                    stat[:, sc_ + 2:sc_ + 3], stat[:, sc_ + 1:sc_ + 2]), after=[t_sm])
                t_pn = P.ve.op(lambda v, gt_=gt_, sc_=sc_: v.tensor_scalar(
                    gt_, gt_, stat[:, sc_ + 2:sc_ + 3], None, OP.mult), after=[t_rc])
                t_pm = P.ve.op(lambda v, gt_=gt_, sc_=sc_: v.reduce_max(
                    stat[:, sc_ + 3:sc_ + 4], gt_, axis=AX), after=[t_pn])
                t_msk = P.ve.op(lambda v, lg=lg, gt_=gt_, sc_=sc_: v.tensor_scalar(
                    lg, gt_, stat[:, sc_ + 3:sc_ + 4], None, OP.is_ge), after=[t_pm])
                t_gt = P.ve.op(lambda v, lg=lg, gt_=gt_: v.tensor_mul(gt_, gt_, lg), after=[t_msk])
                tile_rd_done[t] = t_gt
            gdep = None
            for t in range(NT):
                bkg = pb[6]
                dep = [tile_rd_done[t], t_cm, t_s1]
                if gdep is not None:
                    dep.append(gdep)
                t_gtp = P.pe.op(lambda p, t=t, bkg=bkg: p.transpose(
                    bkg[0:8, 0:128], gts[:, 16 * t + 8:16 * t + 16], idf[:, :]), after=dep)
                t_gh = P.sc.op(lambda s, t=t, bkg=bkg: s.activation(
                    out=gfm[0:8, 128 * t:128 * (t + 1)], in_=bkg[0:8, 0:128], func=AF.Copy),
                    after=[t_gtp])
                t_gl = P.ve.op(lambda v, t=t, bkg=bkg: v.tensor_sub(
                    gfm[0:8, T + 128 * t:T + 128 * (t + 1)], bkg[0:8, 0:128],
                    gfm[0:8, 128 * t:128 * (t + 1)]), after=[t_gh])
                gdep = t_gl
            g_l1 = P.barrier()
        else:
            g_l1 = g_ffn0

        def emit_round(y_ap, q_ap, d_ap, after):
            t0 = P.ve.op(lambda v: v.tensor_scalar(q_ap, y_ap, -31.5, None, OP.max), after=after)
            t0 = P.ve.op(lambda v: v.tensor_scalar(q_ap, q_ap, 31.5, 32.0, OP.min, OP.add), after=[t0])
            t0 = P.ve.op(lambda v: v.tensor_scalar(d_ap, q_ap, 63.5, None, OP.is_ge), after=[t0])
            t0 = P.ve.op(lambda v: v.tensor_scalar(y_ap, d_ap, 64.0, None, OP.mult), after=[t0])
            for bit in [32.0, 16.0, 8.0, 4.0, 2.0, 1.0]:
                t0 = P.ve.op(lambda v: v.tensor_sub(d_ap, q_ap, y_ap), after=[t0])
                t0 = P.ve.op(lambda v, bit=bit: v.tensor_scalar(d_ap, d_ap, bit - 0.5, None, OP.is_ge),
                             after=[t0])
                t0 = P.ve.op(lambda v, bit=bit: v.scalar_tensor_tensor(
                    out=y_ap, in0=d_ap, scalar=bit, in1=y_ap, op0=OP.mult, op1=OP.add), after=[t0])
            t0 = P.ve.op(lambda v: v.tensor_scalar(q_ap, y_ap, -32.0, 31.0, OP.add, OP.min), after=[t0])
            return t0

        def emit_fakequant(src_dram, row0, after):
            wt = regA[:, 0:1024]; yv = regA[:, 1024:2048]
            qb = regB[:, 0:1024]; dd = regB[:, 1024:2048]
            st = lambda i: stat[:, 16 + i:17 + i]
            t_l = sch[0].op(lambda s: s.dma_start(out=wt, in_=src_dram[row0:row0 + 128, :]), after=after)
            t_sq = P.sc.op(lambda s: s.activation(out=dd, in_=wt, func=AF.Square), after=[t_l])
            t_a = P.ve.op(lambda v: v.reduce_sum(st(0), wt, axis=AX), after=[t_l])
            t_b = P.ve.op(lambda v: v.reduce_sum(st(1), dd, axis=AX), after=[t_sq])
            t_a = P.ve.op(lambda v: v.tensor_scalar(st(0), st(0), 1.0 / D, None, OP.mult), after=[t_a])
            t_a = P.ve.op(lambda v: v.tensor_mul(st(2), st(0), st(0)), after=[t_a])
            t_b = P.ve.op(lambda v: v.scalar_tensor_tensor(
                out=st(1), in0=st(1), scalar=1.0 / D, in1=st(2), op0=OP.mult, op1=OP.subtract),
                after=[t_a, t_b])
            t_b = P.sc.op(lambda s: s.activation(out=st(2), in_=st(1), func=AF.Sqrt,
                                                 bias=cvec[:, 4:5], scale=cvec[:, 3:4]), after=[t_b])
            t_b = P.ve.op(lambda v: v.tensor_scalar(st(2), st(2), 1e-8, 12.85, OP.max, OP.mult),
                          after=[t_b])
            t_c = P.ve.op(lambda v: v.reduce_max(st(3), wt, axis=AX, apply_absolute_value=True),
                          after=[t_l])
            t_c = P.ve.op(lambda v: v.tensor_scalar(st(3), st(3), 1.0 / 31.0, 1.0 / 31.0,
                                                    OP.max, OP.mult), after=[t_c])
            t_c = P.ve.op(lambda v: v.reduce_max(st(3), wt, axis=AX, apply_absolute_value=True),
                          after=[t_c])
            t_c = P.ve.op(lambda v: v.tensor_scalar(st(4), st(3), 1.0 / 31.0, 1.0 / 31.0,
                                                    OP.mult, OP.max), after=[t_c])
            t_d = P.ve.op(lambda v: v.tensor_scalar(st(5), st(3), 1.0 / 31.0, None, OP.mult),
                          after=[t_c])
            t_d = P.ve.op(lambda v: v.tensor_max(st(5), st(5), st(2)), after=[t_d, t_b])
            t_d = P.ve.op(lambda v: v.tensor_scalar(st(5), st(5), 1.0 / 31.0, 1.0 / 31.0,
                                                    OP.mult, OP.max), after=[t_d])
            t_e = P.ve.op(lambda v: v.reciprocal(st(6), st(5)), after=[t_d])
            t_e = P.ve.op(lambda v: v.tensor_scalar(yv, wt, st(6), None, OP.mult), after=[t_e])
            t_e = emit_round(yv, qb, dd, [t_e])
            t_e = P.ve.op(lambda v: v.scalar_tensor_tensor(
                out=dd, in0=qb, scalar=st(5), in1=wt, op0=OP.mult, op1=OP.subtract), after=[t_e])
            t_e = P.sc.op(lambda s: s.activation(out=dd, in_=dd, func=AF.Square), after=[t_e])
            t_e1 = P.ve.op(lambda v: v.reduce_sum(st(6), dd, axis=AX), after=[t_e])
            t_f = P.ve.op(lambda v: v.reciprocal(st(7), st(4)), after=[t_e1])
            t_f = P.ve.op(lambda v: v.tensor_scalar(yv, wt, st(7), None, OP.mult), after=[t_f])
            t_f = emit_round(yv, qb, dd, [t_f])
            t_f = P.ve.op(lambda v: v.scalar_tensor_tensor(
                out=dd, in0=qb, scalar=st(4), in1=wt, op0=OP.mult, op1=OP.subtract), after=[t_f])
            t_f = P.sc.op(lambda s: s.activation(out=dd, in_=dd, func=AF.Square), after=[t_f])
            t_e2 = P.ve.op(lambda v: v.reduce_sum(st(7), dd, axis=AX), after=[t_f])
            t_g = P.ve.op(lambda v: v.tensor_tensor(st(8), st(6), st(7), OP.is_lt), after=[t_e1, t_e2])
            t_g = P.ve.op(lambda v: v.tensor_mul(st(9), st(8), st(5)), after=[t_g])
            t_g = P.ve.op(lambda v: v.tensor_scalar(st(10), st(8), -1.0, -1.0, OP.mult, OP.subtract),
                          after=[t_g])
            t_g = P.ve.op(lambda v: v.tensor_mul(st(10), st(10), st(4)), after=[t_g])
            t_g = P.ve.op(lambda v: v.tensor_add(st(9), st(9), st(10)), after=[t_g])
            t_g = P.ve.op(lambda v: v.tensor_copy(s16[:, 0:1], st(9)), after=[t_g])
            t_g = P.ve.op(lambda v: v.tensor_copy(st(9), s16[:, 0:1]), after=[t_g])
            t_h2 = P.ve.op(lambda v: v.reciprocal(st(10), st(9)), after=[t_g])
            t_h2 = P.ve.op(lambda v: v.tensor_scalar(yv, wt, st(10), None, OP.mult), after=[t_h2])
            t_h2 = emit_round(yv, qb, dd, [t_h2])
            t_h2 = P.ve.op(lambda v: v.tensor_scalar(yv, qb, st(9), None, OP.mult), after=[t_h2])
            return yv, t_h2

        def emit_wpairs_from(yv, hpslot, t_src, extra_war=()):
            toks = []
            last_t = None
            for cc in range(8):
                bkb = pb[6 + cc % 2]
                dep = [t_src] + list(extra_war)
                if len(toks) >= 2:
                    dep.append(toks[-2])
                t_tp = P.pe.op(lambda p, bkb=bkb, yv=yv, cc=cc: p.transpose(
                    bkb[:, 0:128], yv[:, 128 * cc:128 * (cc + 1)], idf[:, :]), after=dep)
                t_hi = P.sc.op(lambda s, bkb=bkb, cc=cc, hpslot=hpslot: s.activation(
                    out=HP[:, hpslot + 256 * cc:hpslot + 256 * cc + 128], in_=bkb[:, 0:128],
                    func=AF.Copy), after=[t_tp])
                t_lo = P.ve.op(lambda v, bkb=bkb, cc=cc, hpslot=hpslot: v.tensor_tensor(
                    HP[:, hpslot + 256 * cc + 128:hpslot + 256 * cc + 256], bkb[:, 0:128],
                    HP[:, hpslot + 256 * cc:hpslot + 256 * cc + 128], OP.subtract), after=[t_hi])
                toks.append(t_lo)
                last_t = t_tp
            return toks[-1], last_t

        def l1_qkv_phase(g_prev):
            lrA = WP2[:, 2048:4096]
            la_toks = []
            t_prev = g_prev
            for pi, nm in enumerate(["qA", "kA", "vA", "pA"]):
                dst = pAp[:, 0:512] if pi == 3 else lrA[:, 512 * pi:512 * (pi + 1)]
                t_al = sch[1].op(lambda s, nm=nm: s.dma_start(
                    out=wfs[:, 1024:1280].rearrange("p (a r) -> p a r", a=8),
                    in_=h[nm][:, :].rearrange("(a p) r -> p a r", p=128)), after=[t_prev])
                hi_ap = dst.rearrange("p (k two r) -> p k two r", k=8, two=2)[:, :, 0, :]
                lo_ap = dst.rearrange("p (k two r) -> p k two r", k=8, two=2)[:, :, 1, :]
                t_ah = P.sc.op(lambda s, hi_ap=hi_ap: s.activation(
                    out=hi_ap, in_=wfs[:, 1024:1280].rearrange("p (a r) -> p a r", a=8),
                    func=AF.Copy), after=[t_al])
                t_alo = P.ve.op(lambda v, hi_ap=hi_ap, lo_ap=lo_ap: v.tensor_tensor(
                    lo_ap, wfs[:, 1024:1280].rearrange("p (a r) -> p a r", a=8), hi_ap, OP.subtract),
                    after=[t_ah])
                la_toks.append(t_alo)
                t_prev = t_alo
            xslot = [None, None]
            evdep = [None, None, None]
            xa_wr = []
            for tc in range(NTC):
                mm_last = [None, None, None]
                for k in range(8):
                    xs = k % 2
                    xhi = HP[:, 6144 + 1024 * xs:6144 + 1024 * xs + 512]
                    xlo = HP[:, 6144 + 1024 * xs + 512:6144 + 1024 * xs + 1024]
                    t_x1 = sch[2 + xs].op(lambda s, xhi=xhi, k=k, tc=tc: s.dma_start(
                        out=xhi, in_=scr_xp[0, 128 * k:128 * (k + 1), 512 * tc:512 * (tc + 1)]),
                        after=[g_prev] + ([xslot[xs]] if xslot[xs] else []))
                    t_x2 = sch[2 + xs].op(lambda s, xlo=xlo, k=k, tc=tc: s.dma_start(
                        out=xlo, in_=scr_xp[1, 128 * k:128 * (k + 1), 512 * tc:512 * (tc + 1)]))
                    for pi in range(3):
                        ahi = lrA[:, 512 * pi + 64 * k:512 * pi + 64 * k + 32]
                        alo = lrA[:, 512 * pi + 64 * k + 32:512 * pi + 64 * k + 64]
                        dep0 = [t_x1, t_x2, la_toks[pi]]
                        if k == 0 and evdep[pi] is not None:
                            dep0.append(evdep[pi])
                        for psn, (aop, xop) in enumerate([(ahi, xhi), (alo, xhi), (ahi, xlo)]):
                            mm_last[pi] = P.pe.op(lambda p, b=pb[pi], aop=aop, xop=xop,
                                                  st=(k == 0 and psn == 0), sp=(k == 7 and psn == 2):
                                                  p.matmul(b[0:32, :], aop, xop, start=st, stop=sp),
                                                  after=dep0 if psn == 0 else [])
                    xslot[xs] = mm_last[2]
                for pi in range(3):
                    xsl = stg[0:32, 1024 * pi:1024 * pi + 512]
                    xsll = stg[0:32, 1024 * pi + 512:1024 * pi + 1024]
                    t_ev = P.sc.op(lambda s, b=pb[pi], xsl=xsl: s.activation(
                        out=xsl, in_=b[0:32, :], func=AF.Copy),
                        after=[mm_last[pi]] + (xa_wr[-6:] if tc >= 1 else []))
                    t_evl = P.ve.op(lambda v, b=pb[pi], xsl=xsl, xsll=xsll: v.tensor_tensor(
                        xsll, b[0:32, :], xsl, OP.subtract), after=[t_ev])
                    evdep[pi] = t_evl
                    xa_wr.append(gch[5].op(lambda g_, pi=pi, tc=tc, xsl=xsl: g_.dma_start(
                        out=scr_xa[0, 32 * pi:32 * (pi + 1), 512 * tc:512 * (tc + 1)], in_=xsl),
                        after=[t_ev]))
                    xa_wr.append(gch[5].op(lambda g_, pi=pi, tc=tc, xsll=xsll: g_.dma_start(
                        out=scr_xa[1, 32 * pi:32 * (pi + 1), 512 * tc:512 * (tc + 1)], in_=xsll),
                        after=[t_evl]))
            g_xa = P.barrier()
            def load_bchunk(nm, col0, off, t_after):
                t_bl = sch[1].op(lambda s, nm=nm, col0=col0: s.dma_start(
                    out=wfs[0:32, 1024:1152], in_=h[nm][:, col0:col0 + 128]), after=[t_after])
                t_bh = P.sc.op(lambda s, off=off: s.activation(
                    out=lrB[0:32, off:off + 128], in_=wfs[0:32, 1024:1152], func=AF.Copy),
                    after=[t_bl])
                t_blo = P.ve.op(lambda v, off=off: v.tensor_sub(
                    lrB[0:32, off + 128:off + 256], wfs[0:32, 1024:1152],
                    lrB[0:32, off:off + 128]), after=[t_bh])
                return t_blo
            stgf = stg[:, :].bitcast(F32)
            xslot = [None, None]
            aslot = [None, None]
            pb_free = {}
            scr_done = []
            kst_war = [None]
            vstage_rd = [None]
            vfree = [None, None, None, None]
            vev_prev = [None, None]
            last_split_rd = [g_xa]
            for kvc in range(2):
                t_bkv = load_bchunk("kB", 128 * kvc, 0, g_xa if kvc == 0 else last_split_rd[0])
                t_vblo = load_bchunk("vB", 128 * kvc, 256, t_bkv)
                ydq_k, t_qk = emit_fakequant(h["wk1"], 128 * kvc, [t_vblo] + last_split_rd)
                t_kp, t_kptp = emit_wpairs_from(ydq_k, 0, t_qk)
                ydq_v, t_qv = emit_fakequant(h["wv1"], 128 * kvc, [t_kp, t_kptp])
                t_vp, t_vptp = emit_wpairs_from(ydq_v, 2048, t_qv)
                ev_toks = {0: [], 1: []}
                for tc in range(NTC):
                    banks = [pb[(0 if tc % 2 == 0 else 2) + i] for i in range(2)]
                    for k in range(8):
                        xs = k % 2
                        xhi = HP[:, 6144 + 1024 * xs:6144 + 1024 * xs + 512]
                        xlo = HP[:, 6144 + 1024 * xs + 512:6144 + 1024 * xs + 1024]
                        t_x1 = sch[2 + xs].op(lambda s, xhi=xhi, k=k, tc=tc: s.dma_start(
                            out=xhi, in_=scr_xp[0, 128 * k:128 * (k + 1), 512 * tc:512 * (tc + 1)]),
                            after=[t_vp, t_vptp] + ([xslot[xs]] if xslot[xs] else []))
                        t_x2 = sch[2 + xs].op(lambda s, xlo=xlo, k=k, tc=tc: s.dma_start(
                            out=xlo, in_=scr_xp[1, 128 * k:128 * (k + 1), 512 * tc:512 * (tc + 1)]))
                        lastm = None
                        for pi, hpslot in [(0, 0), (1, 2048)]:
                            whi = HP[:, hpslot + 256 * k:hpslot + 256 * k + 128]
                            wlo = HP[:, hpslot + 256 * k + 128:hpslot + 256 * k + 256]
                            dep0 = [t_x1, t_x2]
                            if k == 0:
                                bkn = (0 if tc % 2 == 0 else 2) + pi
                                if pb_free.get(bkn) is not None:
                                    dep0.append(pb_free[bkn])
                            for psn, (wop, xop) in enumerate([(whi, xhi), (wlo, xhi), (whi, xlo)]):
                                lastm = P.pe.op(lambda p, b=banks[pi], wop=wop, xop=xop,
                                                st=(k == 0 and psn == 0), sp=False:
                                                p.matmul(b[:, :], wop, xop, start=st, stop=sp),
                                                after=dep0 if psn == 0 else [])
                            ev_toks[pi].append(lastm)
                        xslot[xs] = lastm
                    xa_t = []
                    for pi in range(2):
                        base = 4096 + 1024 * pi
                        xa_t.append(sch[4].op(lambda s, tc=tc, pi=pi, base=base: s.dma_start(
                            out=HP[0:32, base:base + 512],
                            in_=scr_xa[0, 32 * (pi + 1):32 * (pi + 2), 512 * tc:512 * (tc + 1)]),
                            after=[g_xa] + ([aslot[0]] if aslot[0] else [])))
                        xa_t.append(sch[4].op(lambda s, tc=tc, pi=pi, base=base: s.dma_start(
                            out=HP[0:32, base + 512:base + 1024],
                            in_=scr_xa[1, 32 * (pi + 1):32 * (pi + 2), 512 * tc:512 * (tc + 1)])))
                    lastm2 = None
                    for pi, bcol in [(0, 0), (1, 256)]:
                        bhi = lrB[0:32, bcol:bcol + 128]
                        blo = lrB[0:32, bcol + 128:bcol + 256]
                        base = 4096 + 1024 * pi
                        xahi = HP[0:32, base:base + 512]
                        xalo = HP[0:32, base + 512:base + 1024]
                        dep0 = xa_t + [t_bkv if pi == 0 else t_vblo]
                        for psn, (bop, xop) in enumerate([(bhi, xahi), (blo, xahi), (bhi, xalo)]):
                            lastm2 = P.pe.op(lambda p, b=banks[pi], bop=bop, xop=xop,
                                             sp=(psn == 2):
                                             p.matmul(b[:, :], bop, xop, start=False, stop=sp),
                                             after=dep0 if psn == 0 else [])
                        ev_toks[pi].append(lastm2)
                    aslot[0] = lastm2
                    for pi in range(2):
                        raw = regB if pi == 0 else regC
                        bkn = (0 if tc % 2 == 0 else 2) + pi
                        t_ev = P.sc.op(lambda s, b=banks[pi], raw=raw, tc=tc: s.activation(
                            out=raw[:, 512 * tc:512 * (tc + 1)], in_=b[:, :], func=AF.Copy),
                            after=[ev_toks[pi][-1]])
                        pb_free[bkn] = t_ev
                        ev_toks[pi].append(t_ev)
                def rope1(raw, evs):
                    t_sh = P.ve.op(lambda v, raw=raw: v.stream_shuffle(stgf[:, 0:T], raw[:, 0:T], _SWAP16),
                                   after=evs)
                    t_s = P.ve.op(lambda v: v.tensor_mul(stgf[:, 0:T], stab[:, 0:T], stgf[:, 0:T]),
                                  after=[t_sh])
                    t_c = P.ve.op(lambda v, raw=raw: v.tensor_mul(raw[:, 0:T], ctab[:, 0:T], raw[:, 0:T]),
                                  after=[t_sh])
                    return P.ve.op(lambda v, raw=raw: v.tensor_add(raw[:, 0:T], raw[:, 0:T], stgf[:, 0:T]),
                                   after=[t_s, t_c])
                t_kr = rope1(regB, [ev_toks[0][-1]])
                wr = []
                q_wr = []
                for hh in range(2):
                    khd = kvc * 2 + hh
                    rs = slice(64 * hh, 64 * hh + 64)
                    t_h2 = P.sc.op(lambda s, rs=rs: s.activation(
                        out=WP[rs, 0:T], in_=regB[rs, 0:T], func=AF.Copy),
                        after=[t_kr] + ([kst_war[0]] if kst_war[0] else []))
                    t_l2 = P.ve.op(lambda v, rs=rs: v.tensor_sub(
                        WP[rs, T:2 * T], regB[rs, 0:T], WP[rs, 0:T]), after=[t_h2])
                    wr.append(gch[4].op(lambda g_, khd=khd, rs=rs: g_.dma_start(
                        out=scr_k[khd, 0:64, :], in_=WP[rs, 0:T]), after=[t_h2]))
                    wr.append(gch[4].op(lambda g_, khd=khd, rs=rs: g_.dma_start(
                        out=scr_k[khd, 64:128, :], in_=WP[rs, T:2 * T]), after=[t_l2]))
                    kst_war[0] = wr[-1]
                t_vh = P.sc.op(lambda s: s.activation(out=stg[:, 0:T], in_=regC[:, 0:T], func=AF.Copy),
                               after=[ev_toks[1][-1], t_kr] + wr)
                t_vl = P.ve.op(lambda v: v.tensor_sub(stg[:, T:2 * T], regC[:, 0:T], stg[:, 0:T]),
                               after=[t_vh])
                for hh in range(2):
                    khd = kvc * 2 + hh
                    rs = slice(64 * hh, 64 * hh + 64)
                    idsl = idb[rs, rs]
                    for kt in range(NT):
                        sl = (hh * NT + kt) % 4
                        b6, b7 = pbb[6], pbb[7]
                        dep = [t_vh, t_vl] + ([vfree[sl]] if vfree[sl] else [])
                        if vev_prev[0] is not None:
                            dep += vev_prev
                        t_t1 = P.pe.op(lambda p, rs=rs, kt=kt, idsl=idsl, sl=sl, b6=b6: p.transpose(
                            b6[:, 128 * sl:128 * sl + 64], stg[rs, 128 * kt:128 * (kt + 1)], idsl),
                            after=dep)
                        t_t2 = P.pe.op(lambda p, rs=rs, kt=kt, idsl=idsl, sl=sl, b7=b7: p.transpose(
                            b7[:, 128 * sl:128 * sl + 64], stg[rs, T + 128 * kt:T + 128 * (kt + 1)], idsl),
                            after=[t_t1])
                        t_e1 = P.sc.op(lambda s, sl=sl, b6=b6: s.activation(
                            out=vstage[:, 130 * sl:130 * sl + 64], in_=b6[:, 128 * sl:128 * sl + 64],
                            func=AF.Copy), after=[t_t1] + ([vfree[sl]] if vfree[sl] else []))
                        t_e2 = P.sc.op(lambda s, sl=sl, b7=b7: s.activation(
                            out=vstage[:, 130 * sl + 65:130 * sl + 129], in_=b7[:, 128 * sl:128 * sl + 64],
                            func=AF.Copy), after=[t_t2] + ([vfree[sl]] if vfree[sl] else []))
                        t_vw = gch[6 + sl].op(lambda g_, khd=khd, kt=kt, sl=sl: g_.dma_start(
                            out=scr_v[khd, 128 * kt:128 * (kt + 1), :], in_=vstage[:, 130 * sl:130 * (sl + 1)]),
                            after=[t_e1, t_e2])
                        vfree[sl] = t_vw
                        wr.append(t_vw)
                        vstage_rd[0] = t_t2
                        vev_prev[0], vev_prev[1] = t_e1, t_e2
                last_split_rd = [vstage_rd[0], t_vl] + wr[-2:]
            g_kv = P.barrier()
            qstg_rd = [[vstage_rd[0], t_vl]]
            t_bq_prev = g_kv
            for g in range(8):
                t_bq = load_bchunk("qB", 128 * g, 0, t_bq_prev)
                ydq_q, t_qq = emit_fakequant(h["wq1"], 128 * g, [t_bq] + last_split_rd)
                t_qp, t_qptp = emit_wpairs_from(ydq_q, 0, t_qq)
                ev_toks = []
                for tc in range(NTC):
                    bank = pb[(tc % 2)]
                    for k in range(8):
                        xs = k % 2
                        xhi = HP[:, 6144 + 1024 * xs:6144 + 1024 * xs + 512]
                        xlo = HP[:, 6144 + 1024 * xs + 512:6144 + 1024 * xs + 1024]
                        t_x1 = sch[2 + xs].op(lambda s, xhi=xhi, k=k, tc=tc: s.dma_start(
                            out=xhi, in_=scr_xp[0, 128 * k:128 * (k + 1), 512 * tc:512 * (tc + 1)]),
                            after=[t_qp, t_qptp] + ([xslot[xs]] if xslot[xs] else []))
                        t_x2 = sch[2 + xs].op(lambda s, xlo=xlo, k=k, tc=tc: s.dma_start(
                            out=xlo, in_=scr_xp[1, 128 * k:128 * (k + 1), 512 * tc:512 * (tc + 1)]))
                        whi = HP[:, 256 * k:256 * k + 128]
                        wlo = HP[:, 256 * k + 128:256 * k + 256]
                        dep0 = [t_x1, t_x2]
                        if k == 0 and pb_free.get(tc % 2) is not None:
                            dep0.append(pb_free[tc % 2])
                        lastm = None
                        for psn, (wop, xop) in enumerate([(whi, xhi), (wlo, xhi), (whi, xlo)]):
                            lastm = P.pe.op(lambda p, b=bank, wop=wop, xop=xop,
                                            st=(k == 0 and psn == 0), sp=False:
                                            p.matmul(b[:, :], wop, xop, start=st, stop=sp),
                                            after=dep0 if psn == 0 else [])
                        xslot[xs] = lastm
                    t_xa1 = sch[4].op(lambda s, tc=tc: s.dma_start(
                        out=HP[0:32, 4096:4608],
                        in_=scr_xa[0, 0:32, 512 * tc:512 * (tc + 1)]),
                        after=[g_xa] + ([aslot[0]] if aslot[0] else []))
                    t_xa2 = sch[4].op(lambda s, tc=tc: s.dma_start(
                        out=HP[0:32, 4608:5120],
                        in_=scr_xa[1, 0:32, 512 * tc:512 * (tc + 1)]))
                    bhi = lrB[0:32, 0:128]
                    blo = lrB[0:32, 128:256]
                    xahi = HP[0:32, 4096:4608]
                    xalo = HP[0:32, 4608:5120]
                    lastm2 = None
                    for psn, (bop, xop) in enumerate([(bhi, xahi), (blo, xahi), (bhi, xalo)]):
                        lastm2 = P.pe.op(lambda p, b=bank, bop=bop, xop=xop, sp=(psn == 2):
                                         p.matmul(b[:, :], bop, xop, start=False, stop=sp),
                                         after=[t_xa1, t_xa2, t_bq] if psn == 0 else [])
                    aslot[0] = lastm2
                    t_ev = P.sc.op(lambda s, b=bank, tc=tc: s.activation(
                        out=regA[:, 512 * tc:512 * (tc + 1)], in_=b[:, :], func=AF.Copy),
                        after=[lastm2])
                    pb_free[tc % 2] = t_ev
                    ev_toks.append(t_ev)
                t_qr = rope1(regA, [ev_toks[-1]])
                for hh in range(2):
                    hd = g * 2 + hh
                    rs = slice(64 * hh, 64 * hh + 64)
                    t_h1 = P.sc.op(lambda s, rs=rs, g=g: s.activation(
                        out=stg[rs, 0:T], in_=regA[rs, 0:T], func=AF.Identity,
                        scale=qgain[rs, g:g + 1], bias=cvec[rs, 4:5]),
                        after=[t_qr] + qstg_rd[0])
                    t_l1 = P.ve.op(lambda v, rs=rs, g=g: v.scalar_tensor_tensor(
                        out=stg[rs, T:2 * T], in0=regA[rs, 0:T], scalar=qgain[rs, g:g + 1],
                        in1=stg[rs, 0:T], op0=OP.mult, op1=OP.subtract), after=[t_h1])
                    for half in range(2):
                        q_wr.append(gch[0].op(lambda g_, hd=hd, rs=rs, half=half: g_.dma_start(
                            out=scr_q[0, hd, 64 * half:64 * half + 64, :], in_=stg[rs, 0:T]), after=[t_h1]))
                        q_wr.append(gch[0].op(lambda g_, hd=hd, rs=rs, half=half: g_.dma_start(
                            out=scr_q[1, hd, 64 * half:64 * half + 64, :], in_=stg[rs, T:2 * T]), after=[t_l1]))
                qstg_rd[0] = q_wr[-8:]
                last_split_rd = q_wr[-8:]
                t_bq_prev = aslot[0]
            return P.barrier()

        def l1_out_phase(g_prev):
            ev2 = None
            ya_wr = []
            for tc in range(NTC):
                lastm = None
                for k in range(8):
                    ahi = pAp[:, 64 * k:64 * k + 32]
                    alo = pAp[:, 64 * k + 32:64 * k + 64]
                    dep0 = [g_prev] if not (tc or k) else []
                    if k == 0 and ev2 is not None:
                        dep0 = dep0 + [ev2]
                    for psn, (aop, xop) in enumerate([
                            (ahi, WP[:, T * k + 512 * tc:T * k + 512 * (tc + 1)]),
                            (alo, WP[:, T * k + 512 * tc:T * k + 512 * (tc + 1)]),
                            (ahi, WP2[:, T * k + 512 * tc:T * k + 512 * (tc + 1)])]):
                        lastm = P.pe.op(lambda p, b=pb[4 + tc % 2], aop=aop, xop=xop,
                                        st=(k == 0 and psn == 0), sp=(k == 7 and psn == 2):
                                        p.matmul(b[0:32, :], aop, xop, start=st, stop=sp),
                                        after=dep0 if psn == 0 else [])
                xsl = stg[0:32, 0:512]
                xsll = stg[0:32, 512:1024]
                t_ev = P.sc.op(lambda s, b=pb[4 + tc % 2], xsl=xsl: s.activation(
                    out=xsl, in_=b[0:32, :], func=AF.Copy),
                    after=[lastm] + ya_wr[-2:])
                t_evl = P.ve.op(lambda v, b=pb[4 + tc % 2], xsl=xsl, xsll=xsll: v.tensor_tensor(
                    xsll, b[0:32, :], xsl, OP.subtract), after=[t_ev])
                ev2 = t_evl
                ya_wr.append(gch[5].op(lambda g_, tc=tc, xsl=xsl: g_.dma_start(
                    out=scr_xa[0, 96:128, 512 * tc:512 * (tc + 1)], in_=xsl), after=[t_ev]))
                ya_wr.append(gch[5].op(lambda g_, tc=tc, xsll=xsll: g_.dma_start(
                    out=scr_xa[1, 96:128, 512 * tc:512 * (tc + 1)], in_=xsll), after=[t_evl]))
            g_ya = P.barrier()
            t_bp_prev = g_ya
            def load_bchunk2(c, t_after):
                t_bl = sch[1].op(lambda s, c=c: s.dma_start(
                    out=wfs[0:32, 1024:1152], in_=h["pB"][:, 128 * c:128 * (c + 1)]), after=[t_after])
                t_bh = P.sc.op(lambda s: s.activation(
                    out=lrB[0:32, 0:128], in_=wfs[0:32, 1024:1152], func=AF.Copy), after=[t_bl])
                return P.ve.op(lambda v: v.tensor_sub(
                    lrB[0:32, 128:256], wfs[0:32, 1024:1152], lrB[0:32, 0:128]), after=[t_bh])
            xadd = None
            bdep = [None, None]
            aslot2 = [None, None]
            last_quant_rd = [g_ya]
            for c in range(8):
                t_bp = load_bchunk2(c, t_bp_prev)
                ydq, t_q = emit_fakequant(h["wo1"], 128 * c, last_quant_rd)
                t_wp, t_wptp = emit_wpairs_from(ydq, 0, t_q)
                for tc in range(NTC):
                    i = c * NTC + tc
                    bkk = pb[i % 2]
                    dep0 = [t_wp, t_wptp]
                    if bdep[i % 2] is not None:
                        dep0.append(bdep[i % 2])
                    t_mm = None
                    for k in range(8):
                        whi = HP[:, 256 * k:256 * k + 128]
                        wlo = HP[:, 256 * k + 128:256 * k + 256]
                        for psn, (wop, xop) in enumerate([
                                (whi, WP[:, T * k + 512 * tc:T * k + 512 * (tc + 1)]),
                                (wlo, WP[:, T * k + 512 * tc:T * k + 512 * (tc + 1)]),
                                (whi, WP2[:, T * k + 512 * tc:T * k + 512 * (tc + 1)])]):
                            t_mm = P.pe.op(lambda p, bkk=bkk, wop=wop, xop=xop,
                                           st=(k == 0 and psn == 0), sp=False:
                                           p.matmul(bkk[:, :], wop, xop, start=st, stop=sp),
                                           after=dep0 if (k == 0 and psn == 0) else [])
                    t_xa1 = sch[4].op(lambda s, tc=tc: s.dma_start(
                        out=HP[0:32, 4096:4608],
                        in_=scr_xa[0, 96:128, 512 * tc:512 * (tc + 1)]),
                        after=ya_wr[-2:] + ([aslot2[0]] if aslot2[0] else []))
                    t_xa2 = sch[4].op(lambda s, tc=tc: s.dma_start(
                        out=HP[0:32, 4608:5120],
                        in_=scr_xa[1, 96:128, 512 * tc:512 * (tc + 1)]))
                    bhi = lrB[0:32, 0:128]
                    blo = lrB[0:32, 128:256]
                    xahi = HP[0:32, 4096:4608]
                    xalo = HP[0:32, 4608:5120]
                    for psn, (bop, xop) in enumerate([(bhi, xahi), (blo, xahi), (bhi, xalo)]):
                        t_mm = P.pe.op(lambda p, bkk=bkk, bop=bop, xop=xop, sp=(psn == 2):
                                       p.matmul(bkk[:, :], bop, xop, start=False, stop=sp),
                                       after=[t_xa1, t_xa2, t_bp] if psn == 0 else [])
                    aslot2[0] = t_mm
                    ofr = regC[:, 512 * (i % 2):512 * (i % 2) + 512]
                    t_ev = P.sc.op(lambda s, bkk=bkk, ofr=ofr: s.activation(
                        out=ofr, in_=bkk[:, :], func=AF.Copy), after=[t_mm])
                    bdep[i % 2] = t_ev
                    tb = pb[4 + i % 2]
                    tps = None
                    for b4 in range(4):
                        dep2 = [t_ev] + ([xadd] if xadd is not None and b4 == 0 else [])
                        tps = P.pe.op(lambda p, tb=tb, ofr=ofr, b4=b4: p.transpose(
                            tb[:, 128 * b4:128 * (b4 + 1)], ofr[:, 128 * b4:128 * (b4 + 1)], idf[:, :]),
                            after=dep2 if b4 == 0 else [])
                    xadd = P.ve.op(lambda v, tb=tb, c=c, tc=tc: v.tensor_add(
                        X[:, :].rearrange("p (n f) -> p n f", n=NT)[:, 4 * tc:4 * tc + 4, 128 * c:128 * (c + 1)],
                        X[:, :].rearrange("p (n f) -> p n f", n=NT)[:, 4 * tc:4 * tc + 4, 128 * c:128 * (c + 1)],
                        tb[:, 0:512].rearrange("p (a f) -> p a f", a=4)), after=[tps])
                last_quant_rd = [t_wp, t_wptp, xadd]
                t_bp_prev = t_mm
            return P.barrier()

        if "l1attn" in phases:
            g_5a = l1_qkv_phase(g_l1)
            g_5b = attn_core_phase(1, g_5a)
            g_l1 = l1_out_phase(g_5b)

        if "l1ffn" in phases:
            g_prev = g_l1
            regCb = regC[:, :].bitcast(BF16)
            gfm = regB[:, :].bitcast(BF16)
            gdep = None
            for t in range(NT):
                bkg = pb[6]
                dep = [g_prev] if gdep is None else [gdep]
                t_gtp = P.pe.op(lambda p, t=t, bkg=bkg: p.transpose(
                    bkg[0:8, 0:128], gts[:, 16 * t + 8:16 * t + 16], idf[:, :]), after=dep)
                t_gh = P.sc.op(lambda s, t=t, bkg=bkg: s.activation(
                    out=gfm[0:8, 128 * t:128 * (t + 1)], in_=bkg[0:8, 0:128], func=AF.Copy),
                    after=[t_gtp])
                t_gl = P.ve.op(lambda v, t=t, bkg=bkg: v.tensor_sub(
                    gfm[0:8, T + 128 * t:T + 128 * (t + 1)], bkg[0:8, 0:128],
                    gfm[0:8, 128 * t:128 * (t + 1)]), after=[t_gh])
                gdep = t_gl
            g_gfm = gdep
            wp_rd = [g_prev]
            wp2_rd = [g_prev]
            wfs_rd = [None, None]
            xsl_rd = [None, None]
            h_rd = {}
            hb_free = [None, None]
            ob_free = [None] * 4
            for e in range(8):
                for qq in range(4):
                    t_w1s = []
                    for k in range(8):
                        ws = k % 2
                        dep = [g_prev] + ([wfs_rd[ws]] if wfs_rd[ws] else [])
                        if k < 2:
                            dep += wp_rd
                        t_wl = sch[ws].op(lambda s, e=e, qq=qq, k=k, ws=ws: s.dma_start(
                            out=wfs[:, 1024 * ws:1024 * (ws + 1)],
                            in_=h["W1"][1024 * e + 128 * k:1024 * e + 128 * (k + 1),
                                        1024 * qq:1024 * (qq + 1)]), after=dep)
                        hi_ap = WP[:, 2048 * k:2048 * (k + 1)].rearrange(
                            "p (ff two f) -> p ff two f", ff=8, two=2)[:, :, 0, :]
                        lo_ap = WP[:, 2048 * k:2048 * (k + 1)].rearrange(
                            "p (ff two f) -> p ff two f", ff=8, two=2)[:, :, 1, :]
                        t_wh = P.sc.op(lambda s, ws=ws, hi_ap=hi_ap: s.activation(
                            out=hi_ap, in_=wfs[:, 1024 * ws:1024 * (ws + 1)].rearrange(
                                "p (a f) -> p a f", a=8), func=AF.Copy), after=[t_wl] + wp_rd)
                        t_wlo = P.ve.op(lambda v, ws=ws, hi_ap=hi_ap, lo_ap=lo_ap: v.tensor_tensor(
                            lo_ap, wfs[:, 1024 * ws:1024 * (ws + 1)].rearrange(
                                "p (a f) -> p a f", a=8), hi_ap, OP.subtract), after=[t_wh])
                        wfs_rd[ws] = t_wlo
                        t_w1s.append(t_wlo)
                    t_w2s = []
                    for ffc in range(8):
                        ws = ffc % 2
                        dep = [g_prev] + ([wfs_rd[ws]] if wfs_rd[ws] else [])
                        if ffc < 2:
                            dep += wp2_rd
                        t_wl = sch[ws].op(lambda s, e=e, qq=qq, ffc=ffc, ws=ws: s.dma_start(
                            out=wfs[:, 1024 * ws:1024 * (ws + 1)],
                            in_=h["W2"][4096 * e + 1024 * qq + 128 * ffc:
                                        4096 * e + 1024 * qq + 128 * (ffc + 1), :]), after=dep)
                        t_wh = P.sc.op(lambda s, ws=ws, ffc=ffc: s.activation(
                            out=WP2[:, 2048 * ffc:2048 * ffc + 1024],
                            in_=wfs[:, 1024 * ws:1024 * (ws + 1)], func=AF.Copy),
                            after=[t_wl] + wp2_rd)
                        t_wlo = P.ve.op(lambda v, ws=ws, ffc=ffc: v.tensor_tensor(
                            WP2[:, 2048 * ffc + 1024:2048 * ffc + 2048],
                            wfs[:, 1024 * ws:1024 * (ws + 1)],
                            WP2[:, 2048 * ffc:2048 * ffc + 1024], OP.subtract), after=[t_wh])
                        wfs_rd[ws] = t_wlo
                        t_w2s.append(t_wlo)
                    wp_rd, wp2_rd = [], []
                    for w in range(NTC):
                        t_xs = []
                        for k in range(8):
                            xc = k % 2
                            dep = [t_w1s[-1]] + ([xsl_rd[xc]] if xsl_rd[xc] else [])
                            t_x1 = sch[2 + xc].op(lambda s, k=k, w=w: s.dma_start(
                                out=HP[:, 1024 * k:1024 * k + 512],
                                in_=scr_xp[0, 128 * k:128 * (k + 1), 512 * w:512 * (w + 1)]),
                                after=dep)
                            t_x2 = sch[2 + xc].op(lambda s, k=k, w=w: s.dma_start(
                                out=HP[:, 1024 * k + 512:1024 * (k + 1)],
                                in_=scr_xp[1, 128 * k:128 * (k + 1), 512 * w:512 * (w + 1)]))
                            t_xs.append(t_x2)
                        hval = {}
                        for ffc in range(8):
                            hb = pb[6 + ffc % 2]
                            t_mm = None
                            for k in range(8):
                                whi = WP[:, 2048 * k + 256 * ffc:2048 * k + 256 * ffc + 128]
                                wlo = WP[:, 2048 * k + 256 * ffc + 128:2048 * k + 256 * ffc + 256]
                                xhi = HP[:, 1024 * k:1024 * k + 512]
                                xlo = HP[:, 1024 * k + 512:1024 * (k + 1)]
                                dep0 = [t_xs[k], t_w1s[k]]
                                if k == 0 and hb_free[ffc % 2] is not None:
                                    dep0.append(hb_free[ffc % 2])
                                elif k == 0 and hb_free[ffc % 2] is None and ffc % 2 == 0:
                                    dep0.append(g_gfm)
                                for psn, (wop, xop) in enumerate([(whi, xhi), (wlo, xhi), (whi, xlo)]):
                                    t_mm = P.pe.op(lambda p, hb=hb, wop=wop, xop=xop,
                                                   st=(k == 0 and psn == 0), sp=(k == 7 and psn == 2):
                                                   p.matmul(hb[:, :], wop, xop, start=st, stop=sp),
                                                   after=dep0 if psn == 0 else [])
                            wp_rd.append(t_mm)
                            hst = regA[:, 512 * (ffc % 2):512 * (ffc % 2) + 512]
                            hreg = stg if ffc < 4 else regCb
                            hcol = 1024 * (ffc % 4)
                            dep_e = [t_mm]
                            if ffc in h_rd:
                                dep_e.append(h_rd[ffc])
                            if ffc >= 2:
                                dep_e.append(hval[ffc - 2][2])
                            t_ev = P.sc.op(lambda s, hb=hb, hst=hst, e=e, qq=qq, ffc=ffc: s.activation(
                                out=hst, in_=hb[:, :], func=AF.Relu,
                                bias=b1_s[:, 32 * e + 8 * qq + ffc:32 * e + 8 * qq + ffc + 1],
                                scale=cvec[:, 3:4]), after=dep_e)
                            hb_free[ffc % 2] = t_ev
                            t_hh = P.sc.op(lambda s, hst=hst, hreg=hreg, hcol=hcol: s.activation(
                                out=hreg[:, hcol:hcol + 512], in_=hst, func=AF.Copy),
                                after=[t_ev] + ([h_rd[ffc]] if ffc in h_rd else []))
                            t_hl = P.ve.op(lambda v, hst=hst, hreg=hreg, hcol=hcol: v.tensor_tensor(
                                hreg[:, hcol + 512:hcol + 1024], hst,
                                hreg[:, hcol:hcol + 512], OP.subtract), after=[t_hh])
                            hval[ffc] = (hreg, hcol, t_hl)
                        xsl_rd[0] = xsl_rd[1] = wp_rd[-1]
                        for t128 in range(4):
                            tile = 4 * w + t128
                            obs = [pb[2 + 2 * (t128 % 2)], pb[3 + 2 * (t128 % 2)]]
                            t_mm2 = [None, None]
                            for ffc in range(8):
                                hreg, hcol, t_hl = hval[ffc]
                                hh_ap = hreg[:, hcol + 128 * t128:hcol + 128 * t128 + 128]
                                hl_ap = hreg[:, hcol + 512 + 128 * t128:hcol + 512 + 128 * t128 + 128]
                                for dh in range(2):
                                    w2hi = WP2[:, 2048 * ffc + 512 * dh:2048 * ffc + 512 * dh + 512]
                                    w2lo = WP2[:, 2048 * ffc + 1024 + 512 * dh:2048 * ffc + 1024 + 512 * dh + 512]
                                    dep0 = [t_hl, t_w2s[ffc]]
                                    if ffc == 0 and ob_free[2 * (t128 % 2) + dh] is not None:
                                        dep0.append(ob_free[2 * (t128 % 2) + dh])
                                    for psn, (hop, wop) in enumerate([(hh_ap, w2hi), (hl_ap, w2hi), (hh_ap, w2lo)]):
                                        t_mm2[dh] = P.pe.op(lambda p, b=obs[dh], hop=hop, wop=wop,
                                                            st=(ffc == 0 and psn == 0), sp=(ffc == 7 and psn == 2):
                                                            p.matmul(b[:, :], hop, wop, start=st, stop=sp),
                                                            after=dep0 if psn == 0 else [])
                                    wp2_rd.append(t_mm2[dh])
                                h_rd[ffc] = t_mm2[1]
                            for dh in range(2):
                                t_add = P.ve.op(lambda v, b=obs[dh], tile=tile, dh=dh, e=e: v.scalar_tensor_tensor(
                                    out=X[:, 1024 * tile + 512 * dh:1024 * tile + 512 * (dh + 1)],
                                    in0=b[:, :], scalar=gts[:, 16 * tile + 8 + e:16 * tile + 9 + e],
                                    in1=X[:, 1024 * tile + 512 * dh:1024 * tile + 512 * (dh + 1)],
                                    op0=OP.mult, op1=OP.add), after=[t_mm2[dh]])
                                ob_free[2 * (t128 % 2) + dh] = t_add
            g_e6 = P.barrier()
            b2p = regA[:, :].bitcast(BF16)[0:8, 2048:4096]
            t_b2l = P.sy.op(lambda s: s.dma_start(out=wfs[0:8, 0:1024], in_=h["b2"][:, :]),
                            after=[g_e6])
            t_b2h = P.sc.op(lambda s: s.activation(out=b2p[:, 0:1024], in_=wfs[0:8, 0:1024],
                                                   func=AF.Copy), after=[t_b2l])
            t_b2lo = P.ve.op(lambda v: v.tensor_sub(b2p[:, 1024:2048], wfs[0:8, 0:1024],
                                                    b2p[:, 0:1024]), after=[t_b2h])
            bdep = None
            for tile in range(NT):
                t_mm = None
                for dh in range(2):
                    bbd = pb[(tile % 2) * 2 + dh]
                    deps = [g_e6] if tile < 2 else ([bdep] if bdep is not None else [])
                    for psn, (gop, bop) in enumerate([
                            (gfm[0:8, 128 * tile:128 * (tile + 1)], b2p[:, 512 * dh:512 * dh + 512]),
                            (gfm[0:8, T + 128 * tile:T + 128 * (tile + 1)], b2p[:, 512 * dh:512 * dh + 512]),
                            (gfm[0:8, 128 * tile:128 * (tile + 1)], b2p[:, 1024 + 512 * dh:1024 + 512 * dh + 512])]):
                        t_mm = P.pe.op(lambda p, bbd=bbd, gop=gop, bop=bop,
                                       st=(psn == 0), sp=(psn == 2):
                                       p.matmul(bbd[:, :], gop, bop, start=st, stop=sp),
                                       after=(deps + [t_b2lo]) if psn == 0 else [])
                    t_ad = P.ve.op(lambda v, bbd=bbd, tile=tile, dh=dh: v.tensor_add(
                        X[:, 1024 * tile + 512 * dh:1024 * tile + 512 * (dh + 1)],
                        X[:, 1024 * tile + 512 * dh:1024 * tile + 512 * (dh + 1)],
                        bbd[:, :]), after=[t_mm])
                    bdep = t_ad

        g_all = P.barrier()
        t_out = P.gp.op(lambda g: g.dma_start(
            out=out_h[:, :].rearrange("(n p) f -> p n f", p=128),
            in_=X[:, :].rearrange("p (n f) -> p n f", n=NT)), after=[g_all])
        P.gp.op(lambda g: g.nop(), after=[t_out], is_dma=False)

        P.finalize()
        with nc.Block() as blk:
            P.emit(blk, sems)
    return nc


def _DBG_SHAPES(T):
    return {"xn": ([T, D], F32), "xq": ([D, T], F32)}



def kernel(**inputs):
    T = 2048
    x = np.ascontiguousarray(np.asarray(inputs["L0_attn_x"], dtype=np.float32))
    g = _host_prepare(T, inputs)
    nc = build_program(T)
    in_maps = [dict(g, x=np.ascontiguousarray(x[b])) for b in range(NCORE)]
    res = run_bass_kernel_spmd(nc, in_maps, core_ids=list(range(NCORE)))
    return np.stack([r["out"] for r in res.results], axis=0).astype(np.float32)
